# version 01
# baseline (speedup 1.0000x reference)
import contextlib
import numpy as np
import ml_dtypes
import concourse.bass as bass
import concourse.mybir as mybir
from concourse.bass_utils import run_bass_kernel_spmd

F32 = mybir.dt.float32
BF16 = mybir.dt.bfloat16
AF = mybir.ActivationFunctionType
ALU = mybir.AluOpType
AX = mybir.AxisListType
BF = ml_dtypes.bfloat16

D = 2048
T = 1024
NCORES = 8
LN_EPS = 1e-5
RMS_EPS = float(np.finfo(np.float32).eps)
QC = 0.5 + 2.0 ** -10
LN_HALF = float(np.log(0.5))


class Sem:
    def __init__(self, es, nc, name):
        self.h = es.enter_context(nc.semaphore(name))
        self.n = 0

    def inc(self, k=1):
        self.n += k
        return self.n


class Plan:
    ENGINES = ("sync", "scalar", "vector", "tensor", "gpsimd")

    def __init__(self):
        self.ops = {e: [] for e in self.ENGINES}

    def on(self, eng):
        def deco(fn):
            self.ops[eng].append(fn)
            return None
        return deco

    def run(self, block):
        for eng in self.ENGINES:
            ops = self.ops[eng]
            if ops:
                getattr(block, eng)(lambda e, _o=tuple(ops): [f(e) for f in _o])


def build_ffn0(debug=False):
    nc = bass.Bass("TRN2", target_bir_lowering=False)
    X = nc.dram_tensor("x", [D, T], F32, kind="ExternalInput").ap()
    WGU = nc.dram_tensor("wgu", [16384, D], F32, kind="ExternalInput").ap()
    WP = nc.dram_tensor("wp", [D, 8192], F32, kind="ExternalInput").ap()
    IDB = nc.dram_tensor("idb", [128, 128], BF16, kind="ExternalInput").ap()
    IDF = nc.dram_tensor("idf", [128, 128], F32, kind="ExternalInput").ap()
    XO = nc.dram_tensor("xo", [D, T], F32, kind="ExternalOutput").ap()
    SCR = nc.dram_tensor("scr", [1024], F32, kind="Internal").ap()
    if debug:
        DBG_Z = nc.dram_tensor("dbg_z", [D, T], BF16, kind="ExternalOutput").ap()
        DBG_H = nc.dram_tensor("dbg_h", [4096, T], BF16, kind="ExternalOutput").ap()
        DBG_R = nc.dram_tensor("dbg_r", [T], F32, kind="ExternalOutput").ap()
        DBG_A = nc.dram_tensor("dbg_a", [D, T], F32, kind="ExternalOutput").ap()

    DC = D // 128
    TBn = T // 512
    NPAIR = 32

    es = contextlib.ExitStack()
    with es:
        z_sb = es.enter_context(nc.sbuf_tensor("z", [128, DC, T], BF16))
        H = es.enter_context(nc.sbuf_tensor("H", [128, 32, T], BF16))
        acc = es.enter_context(nc.sbuf_tensor("acc", [128, DC, T], F32))
        idb = es.enter_context(nc.sbuf_tensor("idb_s", [128, 128], BF16))
        idf = es.enter_context(nc.sbuf_tensor("idf_s", [128, 128], F32))
        ones_k = es.enter_context(nc.sbuf_tensor("ones_k", [128, 1], BF16))
        ones_r = es.enter_context(nc.sbuf_tensor("ones_r", [1, 128], F32))
        xf = [es.enter_context(nc.sbuf_tensor(f"xf{i}", [128, T], F32)) for i in range(2)]
        sq = es.enter_context(nc.sbuf_tensor("sq", [128, T], BF16))
        rrep = es.enter_context(nc.sbuf_tensor("rrep", [128, T], F32))
        gsum = [es.enter_context(nc.sbuf_tensor(f"gsum{i}", [128, 64], F32)) for i in range(2)]
        thrb = [es.enter_context(nc.sbuf_tensor(f"thr{i}", [128, 64], F32)) for i in range(2)]
        scbb = [es.enter_context(nc.sbuf_tensor(f"scb{i}", [128, 64], BF16)) for i in range(2)]
        nthrb = [es.enter_context(nc.sbuf_tensor(f"nthr{i}", [128, 64], F32)) for i in range(2)]
        r2x = es.enter_context(nc.sbuf_tensor("r2x", [128, 8], F32))
        r2row = es.enter_context(nc.sbuf_tensor("r2row", [1, T], F32))
        constv = [LN_EPS, -0.5, LN_HALF]
        cst = es.enter_context(nc.sbuf_tensor("cst", [128, len(constv)], F32))
        for ci, cv in enumerate(constv):
            nc.const_aps.register(F32, cv, cst[:, ci:ci + 1])

        pg = es.enter_context(nc.psum_tensor("pg", [128, T], F32))
        pu = es.enter_context(nc.psum_tensor("pu", [128, T], F32))
        es_ln = contextlib.ExitStack()
        ps_s = [es_ln.enter_context(nc.psum_tensor(f"ps_s{i}", [128, 512], F32)) for i in range(2)]
        ps_q = [es_ln.enter_context(nc.psum_tensor(f"ps_q{i}", [128, 512], F32)) for i in range(2)]
        xb = [es_ln.enter_context(nc.sbuf_tensor(f"xb{i}", [128, T], BF16)) for i in range(2)]
        brep = es_ln.enter_context(nc.sbuf_tensor("brep", [128, T], F32))
        rows = es_ln.enter_context(nc.sbuf_tensor("rows", [1, 3 * T], F32))

        s_ld = Sem(es, nc, "s_ld")
        s_cast = Sem(es, nc, "s_cast")
        s_red = Sem(es, nc, "s_red")
        s_quant = Sem(es, nc, "s_quant")
        s_tp = Sem(es, nc, "s_tp")
        s_evac = Sem(es, nc, "s_evac")
        s_mm = Sem(es, nc, "s_mm")
        s_sil = Sem(es, nc, "s_sil")
        s_hm = Sem(es, nc, "s_hm")
        s_sq = Sem(es, nc, "s_sq")
        s_ss = Sem(es, nc, "s_ss")
        s_stat = Sem(es, nc, "s_stat")
        s_acc = Sem(es, nc, "s_acc")
        s_out = Sem(es, nc, "s_out")
        s_x3 = Sem(es, nc, "s_x3")
        s_init = Sem(es, nc, "s_init")
        s_vch = Sem(es, nc, "s_vch")
        s_sch = Sem(es, nc, "s_sch")
        block = es.enter_context(nc.Block())

        p = Plan()

        @p.on("vector")
        def _(v):
            v.memset(ones_k[:], 1.0)
            v.memset(ones_r[:], 1.0)
            for ci, cv in enumerate(constv):
                v.memset(cst[:, ci:ci + 1], cv)
            v.memset(acc[:], 0.0).then_inc(s_init.h, 1)
        s_init.inc(1)

        @p.on("sync")
        def _(s):
            s.dma_start(out=idb[:], in_=IDB).then_inc(s_ld.h, 16)
            s.dma_start(out=idf[:], in_=IDF).then_inc(s_ld.h, 16)
        id_ld = s_ld.inc(32)

        for dc in range(DC):
            bi = dc % 2
            w_buf = (s_stat.n - 1) if dc >= 2 else None

            @p.on("gpsimd")
            def _(g, dc=dc, bi=bi, w=w_buf):
                if w is not None:
                    g.wait_ge(s_stat.h, w)
                g.dma_start(out=xb[bi][:], in_=X[dc * 128:(dc + 1) * 128, :])\
                    .then_inc(s_cast.h, 16)
            ca = s_cast.inc(16)
            w_sq = s_stat.n if dc >= 1 else None

            @p.on("scalar")
            def _(s, bi=bi, ca=ca, w=w_sq):
                s.wait_ge(s_cast.h, ca)
                if w is not None:
                    s.wait_ge(s_stat.h, w)
                s.activation(sq[:], xb[bi][:], AF.Square).then_inc(s_sq.h, 1)
            qa = s_sq.inc(1)

            @p.on("tensor")
            def _(t, dc=dc, bi=bi, qa=qa):
                t.wait_ge(s_sq.h, qa)
                if dc == 0:
                    t.wait_ge(s_init.h, 1)
                last = None
                for tb in range(TBn):
                    sl = slice(tb * 512, (tb + 1) * 512)
                    t.matmul(ps_s[tb][0:1, :], ones_k[:], xb[bi][:, sl],
                             start=(dc == 0), stop=(dc == DC - 1))
                    last = t.matmul(ps_q[tb][0:1, :], ones_k[:], sq[:, sl],
                                    start=(dc == 0), stop=(dc == DC - 1))
                last.then_inc(s_stat.h, 1)
            s_stat.inc(1)

        row_m = rows[0:1, 0:T]
        row_v = rows[0:1, T:2 * T]
        row_b = rows[0:1, 2 * T:3 * T]
        row_r = row_v
        st_all = s_stat.n

        vch0 = [s_vch.n, s_vch.n + 1]
        s_vch.inc(2)

        @p.on("vector")
        def _(v, w=st_all, c=vch0):
            v.wait_ge(s_stat.h, w)
            last = None
            for tb in range(TBn):
                sl = slice(tb * 512, (tb + 1) * 512)
                v.tensor_scalar(row_m[:, sl], ps_s[tb][0:1, :], 1.0 / D, None, ALU.mult)
                last = v.tensor_scalar(row_v[:, sl], ps_q[tb][0:1, :], 1.0 / D, None, ALU.mult)
            last.then_inc(s_vch.h, 1)
            v.wait_ge(s_vch.h, c[0] + 1)
            v.tensor_tensor(row_b, row_m, row_m, ALU.mult).then_inc(s_vch.h, 1)
            v.wait_ge(s_vch.h, c[1] + 1)
            v.tensor_tensor(row_v, row_v, row_b, ALU.subtract).then_inc(s_stat.h, 1)
        sm1 = s_stat.inc(1)

        sch0 = s_sch.n
        s_sch.inc(1)

        @p.on("scalar")
        def _(s, w=sm1, c=sch0):
            s.wait_ge(s_stat.h, w)
            s.activation(row_b, row_v, AF.Ln, bias=LN_EPS).then_inc(s_sch.h, 1)
            s.wait_ge(s_sch.h, c + 1)
            s.activation(row_r, row_b, AF.Exp, scale=-0.5).then_inc(s_stat.h, 1)
        sm2 = s_stat.inc(1)

        @p.on("vector")
        def _(v, w=sm2):
            v.wait_ge(s_stat.h, w)
            v.scalar_tensor_tensor(row_b, row_m, -1.0, row_r, ALU.mult, ALU.mult)\
                .then_inc(s_stat.h, 1)
        sm3 = s_stat.inc(1)

        @p.on("tensor")
        def _(t, w=sm3, idw=id_ld):
            t.wait_ge(s_stat.h, w)
            t.wait_ge(s_ld.h, idw)
            last = None
            for tb in range(TBn):
                sl = slice(tb * 512, (tb + 1) * 512)
                t.matmul(pg[:, sl], ones_r[:], row_r[:, sl], start=True, stop=True)
                last = t.matmul(pu[:, sl], ones_r[:], row_b[:, sl], start=True, stop=True)
            last.then_inc(s_stat.h, 1)
        sm4 = s_stat.inc(1)

        @p.on("vector")
        def _(v, w=sm4):
            v.wait_ge(s_stat.h, w)
            v.tensor_copy(rrep[:], pg[:, :])
            v.tensor_copy(brep[:], pu[:, :]).then_inc(s_stat.h, 1)
        sm5 = s_stat.inc(1)

        for dc in range(DC):
            bi = dc % 2
            w_buf = (s_stat.n - 1) if dc >= 2 else None

            @p.on("sync")
            def _(s, dc=dc, bi=bi, w=w_buf):
                if w is not None:
                    s.wait_ge(s_stat.h, w)
                s.dma_start(out=xf[bi][:], in_=X[dc * 128:(dc + 1) * 128, :])\
                    .then_inc(s_ld.h, 16)
            la = s_ld.inc(16)

            zc = s_vch.n
            s_vch.inc(1)

            @p.on("vector")
            def _(v, dc=dc, bi=bi, la=la, c=zc, w5=(sm5 if dc == 0 else None)):
                v.wait_ge(s_ld.h, la)
                if w5 is not None:
                    v.wait_ge(s_stat.h, w5)
                v.tensor_tensor(xf[bi][:], xf[bi][:], rrep[:], ALU.mult)\
                    .then_inc(s_vch.h, 1)
                v.wait_ge(s_vch.h, c + 1)
                v.tensor_tensor(z_sb[:, dc, :], xf[bi][:], brep[:], ALU.add)\
                    .then_inc(s_stat.h, 1)
            s_stat.inc(1)
        z_done = s_stat.n
        ln_sm1 = sm1

        es_ln.close()
        ptp = es.enter_context(nc.psum_tensor("ptp", [128, 16 * 128], BF16))
        pss = es.enter_context(nc.psum_tensor("pss", [128, 8], F32))
        wbuf = [es.enter_context(nc.sbuf_tensor(f"wbuf{i}", [128, 2048], BF16)) for i in range(2)]
        whb = [es.enter_context(nc.sbuf_tensor(f"whb{i}", [128, 2048], BF16)) for i in range(2)]
        wqT = [es.enter_context(nc.sbuf_tensor(f"wqT{i}", [128, 16 * 128], BF16)) for i in range(2)]
        sgb = es.enter_context(nc.sbuf_tensor("sgb", [128, T], BF16))

        tp_after_block = []
        wq_last = {0: 0, 1: 0}

        def quant_block(ab, src_dma, ng, dep_cast=None):
            bi = ab % 2
            ncols = ng * 64
            c_wait = (s_quant.n - 1) if ab >= 2 else None
            whb_wait = tp_after_block[ab - 2] if ab >= 2 else None

            @p.on("gpsimd")
            def _(g, bi=bi, w=c_wait, src_dma=src_dma, dep=dep_cast):
                if dep is not None:
                    g.wait_ge(dep[0].h, dep[1])
                if w is not None:
                    g.wait_ge(s_quant.h, w)
                src_dma(g, wbuf[bi])
            ca = s_cast.inc(16)

            @p.on("vector")
            def _(v, ca=ca):
                v.wait_ge(s_cast.h, ca)
                v.nop().then_inc(s_red.h, 1)
            ra = s_red.inc(1)

            qc0 = s_vch.n
            s_vch.inc(7)

            @p.on("vector")
            def _(v, bi=bi, ra=ra, ng=ng, ncols=ncols, ww=whb_wait, c=qc0):
                v.wait_ge(s_red.h, ra)
                if ww:
                    v.wait_ge(s_tp.h, ww)
                gs = gsum[bi][:, 0:ng]
                sc = scbb[bi][:, 0:ng]
                th = thrb[bi][:, 0:ng]
                nt = nthrb[bi][:, 0:ng]
                w3 = wbuf[bi][:, 0:ncols].rearrange("p (g k) -> p g k", k=64)
                h3 = whb[bi][:, 0:ncols].rearrange("p (g k) -> p g k", k=64)
                t3 = th.unsqueeze(2).broadcast_to([128, ng, 64])
                n3 = nt.unsqueeze(2).broadcast_to([128, ng, 64])
                s3 = sc.unsqueeze(2).broadcast_to([128, ng, 64])
                v.tensor_reduce(gs, w3, AX.X, ALU.add, apply_absolute_value=True)\
                    .then_inc(s_vch.h, 1)
                v.wait_ge(s_vch.h, c + 1)
                v.tensor_scalar(sc, gs, 1.0 / 64.0, None, ALU.mult).then_inc(s_vch.h, 1)
                v.wait_ge(s_vch.h, c + 2)
                v.tensor_scalar(sc, sc, 1e-8, None, ALU.max).then_inc(s_vch.h, 1)
                v.wait_ge(s_vch.h, c + 3)
                v.tensor_scalar(th, sc, QC, None, ALU.mult)
                v.tensor_scalar(nt, sc, -QC, None, ALU.mult).then_inc(s_vch.h, 1)
                v.wait_ge(s_vch.h, c + 4)
                v.tensor_tensor(h3, w3, t3, ALU.is_gt).then_inc(s_vch.h, 1)
                v.wait_ge(s_vch.h, c + 5)
                v.tensor_tensor(w3, w3, n3, ALU.is_lt).then_inc(s_vch.h, 1)
                v.wait_ge(s_vch.h, c + 6)
                v.tensor_tensor(h3, h3, w3, ALU.subtract).then_inc(s_vch.h, 1)
                v.wait_ge(s_vch.h, c + 7)
                v.tensor_tensor(h3, h3, s3, ALU.mult).then_inc(s_quant.h, 1)
            return s_quant.inc(1)

        def tp_group(ab, off16, slot, qv, extra_tp_wait=None):
            ev_wait = s_evac.n

            @p.on("tensor")
            def _(t, ab=ab, off16=off16, qv=qv, ev=ev_wait, xtw=extra_tp_wait):
                t.wait_ge(s_quant.h, qv)
                if ev:
                    t.wait_ge(s_evac.h, ev)
                if xtw is not None:
                    t.wait_ge(xtw[0].h, xtw[1])
                wh = whb[ab % 2]
                last = None
                for k in range(16):
                    last = t.transpose(ptp[:, k * 128:(k + 1) * 128],
                                       wh[:, (off16 + k) * 128:(off16 + k + 1) * 128],
                                       idb[:])
                last.then_inc(s_tp.h, 1)
            tv = s_tp.inc(1)
            mm_wait = wq_last[slot]

            @p.on("scalar")
            def _(s, slot=slot, tv=tv, w=mm_wait):
                s.wait_ge(s_tp.h, tv)
                if w:
                    s.wait_ge(s_mm.h, w)
                s.copy(wqT[slot][:], ptp[:]).then_inc(s_evac.h, 1)
            return s_evac.inc(1)

        ab_ctr = [0]

        def emit_phase_a(half):
            for i in range(NPAIR):
                pr = half * NPAIR + i
                for part in range(2):
                    ab = ab_ctr[0]
                    hb = (half * NPAIR + i) + (64 if part else 0)

                    def mk_dma(hb=hb):
                        def f(g, dst):
                            g.dma_start(out=dst[:, 0:2048],
                                        in_=WGU[hb * 128:(hb + 1) * 128, :])\
                                .then_inc(s_cast.h, 16)
                        return f
                    qv = quant_block(ab, mk_dma(), 32,
                                     dep_cast=((s_stat, z_done) if ab == 0 else None))
                    xtw = (s_stat, ln_sm1) if ab == 0 else None
                    ev2 = tp_group(ab, 0, part, qv, extra_tp_wait=xtw)
                    tp_after_block.append(s_tp.n)

                    dst = pg if part == 0 else pu
                    busy = s_sil if part == 0 else s_hm
                    bw = busy.n if pr > 0 else None
                    zw = z_done if pr == 0 and half == 0 else None
                    aw = s_acc.n if (half == 1 and pr == NPAIR and part == 0) else None

                    @p.on("tensor")
                    def _(t, part=part, ev2=ev2, dst=dst, busy=busy, bw=bw,
                          zw=zw, aw=aw):
                        t.wait_ge(s_evac.h, ev2)
                        if zw is not None:
                            t.wait_ge(s_stat.h, zw)
                        if bw:
                            t.wait_ge(busy.h, bw)
                        if aw is not None:
                            t.wait_ge(s_acc.h, aw)
                        last = None
                        for kc in range(DC):
                            for tb in range(TBn):
                                sl = slice(tb * 512, (tb + 1) * 512)
                                last = t.matmul(dst[:, sl],
                                                wqT[part][:, kc * 128:(kc + 1) * 128],
                                                z_sb[:, kc, sl],
                                                start=(kc == 0), stop=(kc == DC - 1))
                        last.then_inc(s_mm.h, 1)
                    s_mm.inc(1)
                    wq_last[part] = s_mm.n
                    ab_ctr[0] += 1

                mmv = s_mm.n
                hw = s_hm.n if pr > 0 else None

                @p.on("scalar")
                def _(s, mmv=mmv, hw=hw):
                    s.wait_ge(s_mm.h, mmv - 1)
                    if hw is not None:
                        s.wait_ge(s_hm.h, hw)
                    s.activation(sgb[:], pg[:, :], AF.Silu).then_inc(s_sil.h, 1)
                sil = s_sil.inc(1)

                @p.on("vector")
                def _(v, i=i, mmv=mmv, sil=sil):
                    v.wait_ge(s_sil.h, sil)
                    v.wait_ge(s_mm.h, mmv)
                    v.tensor_tensor(H[:, i, :], sgb[:], pu[:, :], ALU.mult)\
                        .then_inc(s_hm.h, 1)
                hm = s_hm.inc(1)
                ssw = s_ss.n if pr > 0 else None

                @p.on("scalar")
                def _(s, i=i, hm=hm, ssw=ssw):
                    s.wait_ge(s_hm.h, hm)
                    if ssw is not None:
                        s.wait_ge(s_ss.h, ssw)
                    s.activation(sq[:], H[:, i, :], AF.Square).then_inc(s_sq.h, 1)
                sqv = s_sq.inc(1)

                @p.on("tensor")
                def _(t, pr=pr, sqv=sqv):
                    t.wait_ge(s_sq.h, sqv)
                    last = None
                    for tc in range(8):
                        last = t.matmul(pss[:, tc:tc + 1],
                                        sq[:, tc * 128:(tc + 1) * 128], ones_k[:],
                                        start=(pr == 0 and tc == 0),
                                        stop=(pr == 63 and tc == 7))
                    last.then_inc(s_ss.h, 1)
                s_ss.inc(1)

        def emit_phase_b(half):
            for j in range(DC):
                for sub in range(2):
                    ab = ab_ctr[0]

                    def mk_dma(j=j, half=half, sub=sub):
                        def f(g, dst):
                            g.dma_start(out=dst[:, 0:2048],
                                        in_=WP[j * 128:(j + 1) * 128,
                                               half * 4096 + sub * 2048:
                                               half * 4096 + (sub + 1) * 2048])\
                                .then_inc(s_cast.h, 16)
                        return f
                    qv = quant_block(ab, mk_dma(), 32)
                    tp_group(ab, 0, sub, qv)
                    tp_after_block.append(s_tp.n)
                    ab_ctr[0] += 1
                ev2 = s_evac.n

                hmw = 32 * (half + 1) if j == 0 else None
                silw = 32 * (half + 1) if j == 0 else None
                aw = s_acc.n if (j > 0 or half == 1) else None

                @p.on("tensor")
                def _(t, j=j, ev2=ev2, hmw=hmw, silw=silw, aw=aw):
                    t.wait_ge(s_evac.h, ev2)
                    if hmw is not None:
                        t.wait_ge(s_hm.h, hmw)
                    if silw is not None:
                        t.wait_ge(s_sil.h, silw)
                    if aw is not None:
                        t.wait_ge(s_acc.h, aw)
                    last = None
                    for hc in range(32):
                        w = wqT[hc // 16][:, (hc % 16) * 128:(hc % 16 + 1) * 128]
                        for tb in range(TBn):
                            sl = slice(tb * 512, (tb + 1) * 512)
                            last = t.matmul(pg[:, sl], w, H[:, hc, sl],
                                            start=(hc == 0), stop=(hc == 31))
                    last.then_inc(s_mm.h, 1)
                mmv = s_mm.inc(1)
                wq_last[0] = wq_last[1] = s_mm.n

                @p.on("vector")
                def _(v, j=j, mmv=mmv):
                    v.wait_ge(s_mm.h, mmv)
                    v.tensor_tensor(acc[:, j, :], acc[:, j, :], pg[:, :], ALU.add)\
                        .then_inc(s_acc.h, 1)
                s_acc.inc(1)

        emit_phase_a(0)
        if debug:
            @p.on("gpsimd")
            def _(g, w=z_done, hmw=s_hm.n):
                g.wait_ge(s_stat.h, w)
                g.dma_start(out=DBG_Z.rearrange("(c p) t -> p c t", p=128),
                            in_=z_sb[:]).then_inc(s_ld.h, 16)
                g.wait_ge(s_hm.h, hmw)
                g.dma_start(out=DBG_H.rearrange("(c p) t -> p c t", p=128),
                            in_=H[:]).then_inc(s_ld.h, 16)
            s_ld.inc(32)
        emit_phase_b(0)
        b1_mm = s_mm.n

        @p.on("vector")
        def _(v, w=b1_mm):
            v.wait_ge(s_mm.h, w)
        emit_phase_a(1)
        emit_phase_b(1)

        ss_done = s_ss.n
        acc_done = s_acc.n

        @p.on("vector")
        def _(v, w=ss_done):
            v.wait_ge(s_ss.h, w)
            v.tensor_scalar(r2x[:], pss[:], 1.0 / 8192.0, RMS_EPS, ALU.mult, ALU.add)\
                .then_inc(s_stat.h, 1)
        e1 = s_stat.inc(1)

        sch1 = s_sch.n
        s_sch.inc(1)

        @p.on("scalar")
        def _(s, w=e1, c=sch1):
            s.wait_ge(s_stat.h, w)
            s.activation(r2x[:], r2x[:], AF.Ln).then_inc(s_sch.h, 1)
            s.wait_ge(s_sch.h, c + 1)
            s.activation(r2x[:], r2x[:], AF.Exp, scale=-0.5, bias=LN_HALF)\
                .then_inc(s_stat.h, 1)
        e2 = s_stat.inc(1)

        mm_all = s_mm.n

        @p.on("tensor")
        def _(t, w=e2, mmw=mm_all, idw=id_ld):
            t.wait_ge(s_stat.h, w)
            t.wait_ge(s_mm.h, mmw)
            t.wait_ge(s_ld.h, idw)
            t.transpose(pu[0:8, 0:128], r2x[:], idf[:]).then_inc(s_stat.h, 1)
        e3 = s_stat.inc(1)

        @p.on("vector")
        def _(v, w=e3):
            v.wait_ge(s_stat.h, w)
            v.tensor_copy(rrep[0:8, 0:128], pu[0:8, 0:128]).then_inc(s_stat.h, 1)
        e4 = s_stat.inc(1)

        @p.on("scalar")
        def _(s, w=e4):
            s.wait_ge(s_stat.h, w)
            s.dma_start(out=SCR.rearrange("(a b) -> a b", a=8), in_=rrep[0:8, 0:128])\
                .then_inc(s_ld.h, 16)
        l1 = s_ld.inc(16)

        @p.on("scalar")
        def _(s, w=l1):
            s.wait_ge(s_ld.h, w)
            s.dma_start(out=r2row[:], in_=SCR.rearrange("(a b) -> a b", a=1))\
                .then_inc(s_ld.h, 16)
        l2 = s_ld.inc(16)

        if debug:
            @p.on("gpsimd")
            def _(g, w=l2, aw=s_acc.n):
                g.wait_ge(s_ld.h, w)
                g.dma_start(out=DBG_R.rearrange("t -> 1 t" if False else "(o t) -> o t", o=1),
                            in_=r2row[:]).then_inc(s_ld.h, 16)
                g.wait_ge(s_acc.h, aw)
                g.dma_start(out=DBG_A.rearrange("(c p) t -> p c t", p=128),
                            in_=acc[:]).then_inc(s_ld.h, 16)
            s_ld.inc(32)

        @p.on("tensor")
        def _(t, w=l2):
            t.wait_ge(s_ld.h, w)
            last = None
            for tb in range(TBn):
                sl = slice(tb * 512, (tb + 1) * 512)
                last = t.matmul(pu[:, sl], ones_r[:], r2row[:, sl], start=True, stop=True)
            last.then_inc(s_stat.h, 1)
        e5 = s_stat.inc(1)

        @p.on("vector")
        def _(v, w=e5, aw=acc_done):
            v.wait_ge(s_stat.h, w)
            v.wait_ge(s_acc.h, aw)
            v.tensor_copy(rrep[:], pu[:, :]).then_inc(s_stat.h, 1)
        e6 = s_stat.inc(1)

        for dc in range(DC):
            bi = dc % 2
            ow = (s_out.n - 16) if dc >= 2 else None

            @p.on("sync")
            def _(s, dc=dc, bi=bi, w=e6, ow=ow):
                s.wait_ge(s_stat.h, w)
                if ow is not None:
                    s.wait_ge(s_out.h, ow)
                s.dma_start(out=xf[bi][:], in_=X[dc * 128:(dc + 1) * 128, :])\
                    .then_inc(s_x3.h, 16)
            xa = s_x3.inc(16)

            ec = s_vch.n
            s_vch.inc(1)

            @p.on("vector")
            def _(v, dc=dc, bi=bi, xa=xa, c=ec, w6=(e6 if dc == 0 else None)):
                v.wait_ge(s_x3.h, xa)
                if w6 is not None:
                    v.wait_ge(s_stat.h, w6)
                v.tensor_tensor(acc[:, dc, :], acc[:, dc, :], rrep[:], ALU.mult)\
                    .then_inc(s_vch.h, 1)
                v.wait_ge(s_vch.h, c + 1)
                v.tensor_tensor(xf[bi][:], xf[bi][:], acc[:, dc, :], ALU.add)\
                    .then_inc(s_stat.h, 1)
            za = s_stat.inc(1)

            @p.on("scalar")
            def _(s, dc=dc, bi=bi, za=za):
                s.wait_ge(s_stat.h, za)
                s.dma_start(out=XO[dc * 128:(dc + 1) * 128, :], in_=xf[bi][:])\
                    .then_inc(s_out.h, 16)
            s_out.inc(16)

        @p.on("scalar")
        def _(s, n=s_out.n):
            s.wait_ge(s_out.h, n)

        p.run(block)
    return nc


_CACHE = {}


def _get_nc(name, builder):
    if name not in _CACHE:
        _CACHE[name] = builder()
    return _CACHE[name]


def run_ffn0(x_full, wgu, wp):
    idb = np.eye(128, dtype=BF)
    idf = np.eye(128, dtype=np.float32)
    wgu = np.ascontiguousarray(wgu, dtype=np.float32)
    wp = np.ascontiguousarray(wp, dtype=np.float32)
    in_maps = []
    for c in range(NCORES):
        b, hh = divmod(c, 2)
        shard = np.ascontiguousarray(x_full[b, hh * T:(hh + 1) * T, :].T)
        in_maps.append({"x": shard, "wgu": wgu, "wp": wp, "idb": idb, "idf": idf})
    res = run_bass_kernel_spmd(_get_nc("ffn0", build_ffn0), in_maps,
                               core_ids=list(range(NCORES)))
    out = np.empty_like(x_full)
    for c in range(NCORES):
        b, hh = divmod(c, 2)
        out[b, hh * T:(hh + 1) * T, :] = res.results[c]["xo"].T
    return out




def build_ffn1(debug=False, upto=None, kill=()):
    nc = bass.Bass("TRN2", target_bir_lowering=False)
    X = nc.dram_tensor("x", [D, T], F32, kind="ExternalInput").ap()
    WBT = nc.dram_tensor("wbt", [D, D], F32, kind="ExternalInput").ap()
    BB = nc.dram_tensor("bb", [128, 16], F32, kind="ExternalInput").ap()
    RWT = nc.dram_tensor("rwt", [D, 8], F32, kind="ExternalInput").ap()
    AST = nc.dram_tensor("ast", [D, 128], F32, kind="ExternalInput").ap()
    BWT = nc.dram_tensor("bwt", [128, D], F32, kind="ExternalInput").ap()
    REP2 = nc.dram_tensor("rep2", [128, 128], F32, kind="ExternalInput").ap()
    IDB = nc.dram_tensor("idb", [128, 128], BF16, kind="ExternalInput").ap()
    IDF = nc.dram_tensor("idf", [128, 128], F32, kind="ExternalInput").ap()
    XO = nc.dram_tensor("xo", [D, T], F32, kind="ExternalOutput").ap()
    if debug:
        DBG_Z = nc.dram_tensor("dbg_z", [D, T], BF16, kind="ExternalOutput").ap()
        DBG_MF = nc.dram_tensor("dbg_mf", [8, T], F32, kind="ExternalOutput").ap()
        DBG_M01 = nc.dram_tensor("dbg_m01", [128, 64], F32, kind="ExternalOutput").ap()
        DBG_MX = nc.dram_tensor("dbg_mx", [128, 8], F32, kind="ExternalOutput").ap()
        DBG_MR = nc.dram_tensor("dbg_mr", [128, T], BF16, kind="ExternalOutput").ap()
        DBG_HM = nc.dram_tensor("dbg_hm", [128, T], BF16, kind="ExternalOutput").ap()

    DC = D // 128
    TBn = T // 512

    es = contextlib.ExitStack()
    with es:
        z_sb = es.enter_context(nc.sbuf_tensor("z", [128, DC + 1, T], BF16))
        wb_c = [es.enter_context(nc.sbuf_tensor(f"wb_c{i}", [128, D], BF16)) for i in range(DC)]
        ast_c = [es.enter_context(nc.sbuf_tensor(f"ast_c{i}", [128, 128], BF16)) for i in range(DC)]
        bwt_sb = es.enter_context(nc.sbuf_tensor("bwt_sb", [128, D], BF16))
        rwt_c = [es.enter_context(nc.sbuf_tensor(f"rwt_c{i}", [128, 8], BF16)) for i in range(DC)]
        rep2_sb = es.enter_context(nc.sbuf_tensor("rep2_sb", [128, 128], F32))
        bb_sb = es.enter_context(nc.sbuf_tensor("bb_sb", [128, 16], F32))
        idb = es.enter_context(nc.sbuf_tensor("idb_s", [128, 128], BF16))
        idf = es.enter_context(nc.sbuf_tensor("idf_s", [128, 128], F32))
        ones_k = es.enter_context(nc.sbuf_tensor("ones_k", [128, 1], BF16))
        ones_r = es.enter_context(nc.sbuf_tensor("ones_r", [1, 128], F32))
        xb = [es.enter_context(nc.sbuf_tensor(f"xb{i}", [128, T], BF16)) for i in range(2)]
        xf = [es.enter_context(nc.sbuf_tensor(f"xf{i}", [128, T], F32)) for i in range(4)]
        sq = es.enter_context(nc.sbuf_tensor("sq", [128, T], BF16))
        rows = es.enter_context(nc.sbuf_tensor("rows", [1, 3 * T], F32))
        rrep = es.enter_context(nc.sbuf_tensor("rrep", [128, T], F32))
        brep = es.enter_context(nc.sbuf_tensor("brep", [128, T], F32))
        xe = [es.enter_context(nc.sbuf_tensor(f"xe{i}", [128, T], F32)) for i in range(2)]
        mx8 = es.enter_context(nc.sbuf_tensor("mx8", [128, 8], F32))
        m01 = es.enter_context(nc.sbuf_tensor("m01", [128, 64], F32))
        maskf = es.enter_context(nc.sbuf_tensor("maskf", [8, T], F32))
        mrep = es.enter_context(nc.sbuf_tensor("mrep", [128, T], BF16))
        hm = es.enter_context(nc.sbuf_tensor("hm", [128, T], BF16))
        constv = [LN_EPS, -0.5]
        cst = es.enter_context(nc.sbuf_tensor("cst", [128, len(constv)], F32))
        for ci, cv in enumerate(constv):
            nc.const_aps.register(F32, cv, cst[:, ci:ci + 1])

        pg = es.enter_context(nc.psum_tensor("pg", [128, T], F32))
        pu = es.enter_context(nc.psum_tensor("pu", [128, T], F32))
        es_ln = contextlib.ExitStack()
        ps_s = [es_ln.enter_context(nc.psum_tensor(f"ps_s{i}", [128, 512], F32)) for i in range(2)]
        ps_q = [es_ln.enter_context(nc.psum_tensor(f"ps_q{i}", [128, 512], F32)) for i in range(2)]

        s_ld = Sem(es, nc, "s_ld")
        s_cast = Sem(es, nc, "s_cast")
        s_sq = Sem(es, nc, "s_sq")
        s_stat = Sem(es, nc, "s_stat")
        s_mm = Sem(es, nc, "s_mm")
        s_ev = Sem(es, nc, "s_ev")
        s_out = Sem(es, nc, "s_out")
        s_x3 = Sem(es, nc, "s_x3")
        s_init = Sem(es, nc, "s_init")
        s_vch = Sem(es, nc, "s_vch")
        s_sch = Sem(es, nc, "s_sch")
        block = es.enter_context(nc.Block())
        p = Plan()
        s_out_n = [0]

        @p.on("vector")
        def _(v):
            v.memset(ones_k[:], 1.0)
            v.memset(ones_r[:], 1.0)
            for ci, cv in enumerate(constv):
                v.memset(cst[:, ci:ci + 1], cv)
            v.memset(ones_r[:], 1.0).then_inc(s_init.h, 1)
        s_init.inc(1)

        wload = [0]

        @p.on("sync")
        def _(s):
            s.dma_start(out=bb_sb[:], in_=BB).then_inc(s_ld.h, 16)
            s.dma_start(out=idb[:], in_=IDB).then_inc(s_ld.h, 16)
            s.dma_start(out=idf[:], in_=IDF).then_inc(s_ld.h, 16)
        id_ld = s_ld.inc(48)

        for dc in range(DC):
            bi = dc % 2
            w_buf = (s_stat.n - 1) if dc >= 2 else None

            @p.on("gpsimd")
            def _(g, dc=dc, bi=bi, w=w_buf):
                if w is not None:
                    g.wait_ge(s_stat.h, w)
                g.dma_start(out=xb[bi][:], in_=X[dc * 128:(dc + 1) * 128, :])                    .then_inc(s_cast.h, 16)
            ca = s_cast.inc(16)
            w_sq = s_stat.n if dc >= 1 else None

            @p.on("scalar")
            def _(s, bi=bi, ca=ca, w=w_sq):
                s.wait_ge(s_cast.h, ca)
                if w is not None:
                    s.wait_ge(s_stat.h, w)
                s.activation(sq[:], xb[bi][:], AF.Square).then_inc(s_sq.h, 1)
            qa = s_sq.inc(1)

            @p.on("tensor")
            def _(t, dc=dc, bi=bi, qa=qa):
                t.wait_ge(s_sq.h, qa)
                if dc == 0:
                    t.wait_ge(s_init.h, 1)
                last = None
                for tb in range(TBn):
                    sl = slice(tb * 512, (tb + 1) * 512)
                    t.matmul(ps_s[tb][0:1, :], ones_k[:], xb[bi][:, sl],
                             start=(dc == 0), stop=(dc == DC - 1))
                    last = t.matmul(ps_q[tb][0:1, :], ones_k[:], sq[:, sl],
                                    start=(dc == 0), stop=(dc == DC - 1))
                last.then_inc(s_stat.h, 1)
            s_stat.inc(1)

        row_m = rows[0:1, 0:T]
        row_v = rows[0:1, T:2 * T]
        row_b = rows[0:1, 2 * T:3 * T]
        row_r = row_v
        st_all = s_stat.n
        vch0 = [s_vch.n, s_vch.n + 1]
        s_vch.inc(2)

        @p.on("vector")
        def _(v, w=st_all, c=vch0):
            v.wait_ge(s_stat.h, w)
            last = None
            for tb in range(TBn):
                sl = slice(tb * 512, (tb + 1) * 512)
                v.tensor_scalar(row_m[:, sl], ps_s[tb][0:1, :], 1.0 / D, None, ALU.mult)
                last = v.tensor_scalar(row_v[:, sl], ps_q[tb][0:1, :], 1.0 / D, None, ALU.mult)
            last.then_inc(s_vch.h, 1)
            v.wait_ge(s_vch.h, c[0] + 1)
            v.tensor_tensor(row_b, row_m, row_m, ALU.mult).then_inc(s_vch.h, 1)
            v.wait_ge(s_vch.h, c[1] + 1)
            v.tensor_tensor(row_v, row_v, row_b, ALU.subtract).then_inc(s_stat.h, 1)
        sm1 = s_stat.inc(1)
        sch0 = s_sch.n
        s_sch.inc(1)

        @p.on("scalar")
        def _(s, w=sm1, c=sch0):
            s.wait_ge(s_stat.h, w)
            s.activation(row_b, row_v, AF.Ln, bias=LN_EPS).then_inc(s_sch.h, 1)
            s.wait_ge(s_sch.h, c + 1)
            s.activation(row_r, row_b, AF.Exp, scale=-0.5).then_inc(s_stat.h, 1)
        sm2 = s_stat.inc(1)

        @p.on("vector")
        def _(v, w=sm2):
            v.wait_ge(s_stat.h, w)
            v.scalar_tensor_tensor(row_b, row_m, -1.0, row_r, ALU.mult, ALU.mult)                .then_inc(s_stat.h, 1)
        sm3 = s_stat.inc(1)

        @p.on("tensor")
        def _(t, w=sm3, idw=id_ld):
            t.wait_ge(s_stat.h, w)
            t.wait_ge(s_ld.h, idw)
            last = None
            for tb in range(TBn):
                sl = slice(tb * 512, (tb + 1) * 512)
                t.matmul(pg[:, sl], ones_r[:], row_r[:, sl], start=True, stop=True)
                last = t.matmul(pu[:, sl], ones_r[:], row_b[:, sl], start=True, stop=True)
            last.then_inc(s_stat.h, 1)
        sm4 = s_stat.inc(1)

        @p.on("vector")
        def _(v, w=sm4):
            v.wait_ge(s_stat.h, w)
            v.tensor_copy(rrep[:], pg[:, :])
            v.tensor_copy(brep[:], pu[:, :]).then_inc(s_stat.h, 1)
        sm5 = s_stat.inc(1)

        for dc in range(DC):
            bi = dc % 2
            w_buf = (s_stat.n - 1) if dc >= 2 else None

            @p.on("sync")
            def _(s, dc=dc, bi=bi, w=w_buf):
                if w is not None:
                    s.wait_ge(s_stat.h, w)
                s.dma_start(out=xf[bi][:], in_=X[dc * 128:(dc + 1) * 128, :])                    .then_inc(s_ld.h, 16)
            la = s_ld.inc(16)
            zc = s_vch.n
            s_vch.inc(1)

            @p.on("vector")
            def _(v, dc=dc, bi=bi, la=la, c=zc, w5=(sm5 if dc == 0 else None)):
                v.wait_ge(s_ld.h, la)
                if w5 is not None:
                    v.wait_ge(s_stat.h, w5)
                v.tensor_tensor(xf[bi][:], xf[bi][:], rrep[:], ALU.mult)                    .then_inc(s_vch.h, 1)
                v.wait_ge(s_vch.h, c + 1)
                v.tensor_tensor(z_sb[:, dc, :], xf[bi][:], brep[:], ALU.add)                    .then_inc(s_stat.h, 1)
            s_stat.inc(1)
        zw0 = s_stat.n

        @p.on("vector")
        def _(v, w=zw0):
            v.wait_ge(s_stat.h, w)
            v.memset(mx8[:, 0:1], 0.0).then_inc(s_stat.h, 1)
        z_done = s_stat.inc(1)

        @p.on("gpsimd")
        def _(g, en=(upto != "z_noweights"), base=s_cast.n):
            if not en:
                return
            n = 0
            def th(inst):
                nonlocal n
                n += 1
                if n % 4 == 0:
                    g.wait_ge(s_cast.h, base + (n - 3) * 16)
            for c in range(16):
                th(g.dma_start(out=wb_c[c][:], in_=WBT[c * 128:(c + 1) * 128, :]).then_inc(s_cast.h, 16))
                th(g.dma_start(out=ast_c[c][:], in_=AST[c * 128:(c + 1) * 128, :]).then_inc(s_cast.h, 16))
                th(g.dma_start(out=rwt_c[c][:], in_=RWT[c * 128:(c + 1) * 128, :]).then_inc(s_cast.h, 16))
            th(g.dma_start(out=bwt_sb[:], in_=BWT).then_inc(s_cast.h, 16))
            th(g.dma_start(out=rep2_sb[:], in_=REP2).then_inc(s_cast.h, 16))
        w_ld = s_cast.inc(16 * 50)
        if upto in ("z", "z_noweights"):
            es_ln.close()
            @p.on("gpsimd")
            def _(g, w=z_done):
                g.wait_ge(s_stat.h, w)
                for dcq in range(DC):
                    g.dma_start(out=XO[dcq * 128:(dcq + 1) * 128, :],
                                in_=z_sb[:, dcq, :]).then_inc(s_out.h, 16)
                g.wait_ge(s_out.h, s_out_n[0] + 256)
            s_out_n[0] += 256
            p.run(block)
            return nc
        es_ln.close()
        plg = es.enter_context(nc.psum_tensor("plg", [128, 64], F32))
        pmk = es.enter_context(nc.psum_tensor("pmk", [8, T], F32))

        @p.on("tensor")
        def _(t, w=z_done, wl=w_ld, kl=("lg" in kill)):
            t.wait_ge(s_stat.h, w)
            t.wait_ge(s_cast.h, wl)
            if kl:
                t.nop().then_inc(s_mm.h, 1)
                return
            last = None
            for tc in range(8):
                for kc in range(DC):
                    last = t.matmul(plg[:, tc * 8:(tc + 1) * 8],
                                    z_sb[:, kc, tc * 128:(tc + 1) * 128],
                                    rwt_c[kc][:],
                                    start=(kc == 0), stop=(kc == DC - 1))
            last.then_inc(s_mm.h, 1)
        lg_mm = s_mm.inc(1)
        vch1 = s_vch.n
        s_vch.inc(1)
        KP = "post" in kill

        @p.on("vector")
        def _(v, w=lg_mm, c=vch1):
            v.wait_ge(s_mm.h, w)
            if KP:
                v.nop().then_inc(s_vch.h, 1)
                v.nop().then_inc(s_stat.h, 1)
                return
            v.tensor_reduce(mx8[:], plg[:].rearrange("p (c e) -> p c e", e=8),
                            AX.X, ALU.max).then_inc(s_vch.h, 1)
            v.wait_ge(s_vch.h, c + 1)
            l3 = plg[:].rearrange("p (c e) -> p c e", e=8)
            x3v = mx8[:].unsqueeze(2).broadcast_to([128, 8, 8])
            m3 = m01[:].rearrange("p (c e) -> p c e", e=8)
            v.tensor_tensor(m3, l3, x3v, ALU.is_ge).then_inc(s_stat.h, 1)
        mk1 = s_stat.inc(1)

        @p.on("tensor")
        def _(t, w=mk1):
            t.wait_ge(s_stat.h, w)
            if KP:
                t.nop().then_inc(s_mm.h, 1)
                return
            last = None
            for tc in range(8):
                last = t.transpose(pmk[:, tc * 128:(tc + 1) * 128],
                                   m01[:, tc * 8:(tc + 1) * 8], idf[:])
            last.then_inc(s_mm.h, 1)
        mk2 = s_mm.inc(1)

        @p.on("vector")
        def _(v, w=mk2):
            v.wait_ge(s_mm.h, w)
            if KP:
                v.nop().then_inc(s_stat.h, 1)
                return
            v.tensor_copy(maskf[:], pmk[:]).then_inc(s_stat.h, 1)
        mk3 = s_stat.inc(1)

        @p.on("tensor")
        def _(t, w=mk3):
            t.wait_ge(s_stat.h, w)
            if KP:
                t.nop().then_inc(s_mm.h, 1)
                return
            last = None
            for tb in range(TBn):
                sl = slice(tb * 512, (tb + 1) * 512)
                last = t.matmul(pg[:, sl], rep2_sb[0:8, :], maskf[:, sl],
                                start=True, stop=True)
            last.then_inc(s_mm.h, 1)
        mk4 = s_mm.inc(1)

        @p.on("vector")
        def _(v, w=mk4):
            v.wait_ge(s_mm.h, w)
            if KP:
                v.nop().then_inc(s_stat.h, 1)
                return
            v.tensor_copy(mrep[:], pg[:, :]).then_inc(s_stat.h, 1)
        mk5 = s_stat.inc(1)

        @p.on("tensor")
        def _(t, w=mk5):
            t.wait_ge(s_stat.h, w)
            if KP:
                t.nop().then_inc(s_mm.h, 1)
                return
            last = None
            for tb in range(TBn):
                sl = slice(tb * 512, (tb + 1) * 512)
                for kc in range(DC):
                    last = t.matmul(pu[:, sl], ast_c[kc][:], z_sb[:, kc, sl],
                                    start=(kc == 0), stop=(kc == DC - 1))
            last.then_inc(s_mm.h, 1)
        ha_mm = s_mm.inc(1)

        @p.on("vector")
        def _(v, w=ha_mm):
            v.wait_ge(s_mm.h, w)
            if KP:
                v.nop().then_inc(s_stat.h, 1)
                return
            v.tensor_tensor(hm[:], pu[:, :], mrep[:], ALU.mult).then_inc(s_stat.h, 1)
        hm_done = s_stat.inc(1)

        for j in range(DC):
            bi = j % 2
            mm_w = hm_done

            @p.on("tensor")
            def _(t, j=j, w=mm_w, ew=(s_ev.n if j >= 1 else None), kb=("base" in kill)):
                t.wait_ge(s_stat.h, w)
                if ew:
                    t.wait_ge(s_ev.h, ew)
                if kb:
                    t.nop().then_inc(s_mm.h, 1)
                    return
                last = None
                for tb in range(TBn):
                    sl = slice(tb * 512, (tb + 1) * 512)
                    for kc in range(DC):
                        t.matmul(pg[:, sl], wb_c[kc][:, j * 128:(j + 1) * 128],
                                 z_sb[:, kc, sl], start=(kc == 0), stop=False)
                    last = t.matmul(pg[:, sl], bwt_sb[:, j * 128:(j + 1) * 128],
                                    hm[:, sl], start=False, stop=True)
                last.then_inc(s_mm.h, 1)
            mmv = s_mm.inc(1)

            @p.on("sync")
            def _(s, j=j, bi=bi, w=(s_out.n - 16 if j >= 2 else None),
                  zg=(z_done if j < 2 else None)):
                if zg is not None:
                    s.wait_ge(s_stat.h, zg)
                if w is not None:
                    s.wait_ge(s_out.h, w)
                s.dma_start(out=xf[bi][:], in_=X[j * 128:(j + 1) * 128, :])\
                    .then_inc(s_x3.h, 16)
            xa = s_x3.inc(16)
            ec = s_vch.n
            s_vch.inc(1)

            @p.on("vector")
            def _(v, j=j, bi=bi, w=mmv, xa=xa, c=ec,
                  evp=(s_ev.n if j >= 1 else None)):
                v.wait_ge(s_mm.h, w)
                v.wait_ge(s_x3.h, xa)
                if evp:
                    v.wait_ge(s_ev.h, evp)
                v.tensor_scalar(rrep[:], pg[:, :], 0.5, bb_sb[:, j:j + 1],
                                ALU.mult, ALU.add).then_inc(s_vch.h, 1)
                v.wait_ge(s_vch.h, c + 1)
                v.tensor_tensor(xf[bi][:], xf[bi][:], rrep[:], ALU.add)\
                    .then_inc(s_ev.h, 1)
            za = s_ev.inc(1)

            @p.on("scalar")
            def _(s, j=j, bi=bi, za=za):
                s.wait_ge(s_ev.h, za)
                s.dma_start(out=XO[j * 128:(j + 1) * 128, :], in_=xf[bi][:])\
                    .then_inc(s_out.h, 16)
            s_out.inc(16)

        if debug:
            dump_base = s_ld.n

            @p.on("gpsimd")
            def _(g, w=s_out.n, wv=dump_base + 96):
                g.wait_ge(s_out.h, w)
                g.dma_start(out=DBG_Z.rearrange("(c p) t -> p c t", p=128),
                            in_=z_sb[:, 0:DC, :]).then_inc(s_ld.h, 16)
                g.dma_start(out=DBG_MF, in_=maskf[:]).then_inc(s_ld.h, 16)
                g.dma_start(out=DBG_MR, in_=mrep[:]).then_inc(s_ld.h, 16)
                g.dma_start(out=DBG_HM, in_=hm[:]).then_inc(s_ld.h, 16)
                g.dma_start(out=DBG_M01, in_=m01[:]).then_inc(s_ld.h, 16)
                g.dma_start(out=DBG_MX, in_=mx8[:]).then_inc(s_ld.h, 16)
                g.wait_ge(s_ld.h, wv)
            s_ld.inc(96)

        @p.on("scalar")
        def _(s, n=s_out.n):
            s.wait_ge(s_out.h, n)

        p.run(block)
    return nc


def run_ffn1(x_full, W_base, b_base, router_W, A, Bw):
    idb = np.eye(128, dtype=BF)
    idf = np.eye(128, dtype=np.float32)
    wbT = np.ascontiguousarray(np.asarray(W_base, np.float32).T)
    bb = np.ascontiguousarray(np.asarray(b_base, np.float32).reshape(16, 128).T)
    rwT = np.ascontiguousarray(np.asarray(router_W, np.float32).T)
    ast = np.ascontiguousarray(np.asarray(A, np.float32).transpose(2, 0, 1).reshape(D, 128))
    bwT = np.ascontiguousarray(np.asarray(Bw, np.float32).transpose(0, 2, 1).reshape(128, D))
    rep2 = np.zeros((128, 128), dtype=np.float32)
    for e in range(8):
        rep2[e, e * 16:(e + 1) * 16] = 2.0
    in_maps = []
    for c in range(NCORES):
        b, hh = divmod(c, 2)
        shard = np.ascontiguousarray(x_full[b, hh * T:(hh + 1) * T, :].T)
        in_maps.append({"x": shard, "wbt": wbT, "bb": bb, "rwt": rwT, "ast": ast,
                        "bwt": bwT, "rep2": rep2, "idb": idb, "idf": idf})
    res = run_bass_kernel_spmd(_get_nc("ffn1", build_ffn1), in_maps,
                               core_ids=list(range(NCORES)))
    out = np.empty_like(x_full)
    for c in range(NCORES):
        b, hh = divmod(c, 2)
        out[b, hh * T:(hh + 1) * T, :] = res.results[c]["xo"].T
    return out




def build_attn(upto=None):
    nc = bass.Bass("TRN2", target_bir_lowering=False)
    T2 = 2048
    TO = 1024
    DC = 16
    SCALE = 0.08838834764831845
    XF = nc.dram_tensor("xf", [D, T2], F32, kind="ExternalInput").ap()
    WQ3 = nc.dram_tensor("wq3", [16, D, 128], F32, kind="ExternalInput").ap()
    WK3 = nc.dram_tensor("wk3", [4, D, 128], F32, kind="ExternalInput").ap()
    WV3 = nc.dram_tensor("wv3", [4, D, 128], F32, kind="ExternalInput").ap()
    WO3 = nc.dram_tensor("wo3", [16, D, 128], F32, kind="ExternalInput").ap()
    COS = nc.dram_tensor("cos", [64, T2], BF16, kind="ExternalInput").ap()
    SIN = nc.dram_tensor("sin", [64, T2], BF16, kind="ExternalInput").ap()
    IDF = nc.dram_tensor("idf", [128, 128], F32, kind="ExternalInput").ap()
    IDB = nc.dram_tensor("idb", [128, 128], BF16, kind="ExternalInput").ap()
    XO = nc.dram_tensor("xo", [D, TO], F32, kind="ExternalOutput").ap()
    SCR2 = nc.dram_tensor("scr2", [4096], F32, kind="Internal").ap()

    es = contextlib.ExitStack()
    with es:
        q_sb = es.enter_context(nc.sbuf_tensor("q_sb", [128, 16, TO], BF16))
        k_sb = es.enter_context(nc.sbuf_tensor("k_sb", [128, 4, T2], BF16))
        vT = es.enter_context(nc.sbuf_tensor("vT", [128, 4, 16, 128], BF16))
        vtmp = es.enter_context(nc.sbuf_tensor("vtmp", [128, T2], BF16))
        cosb = es.enter_context(nc.sbuf_tensor("cosb", [64, T2], BF16))
        sinb = es.enter_context(nc.sbuf_tensor("sinb", [64, T2], BF16))
        idf = es.enter_context(nc.sbuf_tensor("idf_s", [128, 128], F32))
        idb = es.enter_context(nc.sbuf_tensor("idb_s", [128, 128], BF16))
        ones_k = es.enter_context(nc.sbuf_tensor("ones_k", [128, 1], BF16))
        ones_r = es.enter_context(nc.sbuf_tensor("ones_r", [1, 128], F32))
        rb = es.enter_context(nc.sbuf_tensor("rb", [128, 48], F32))
        st32 = es.enter_context(nc.sbuf_tensor("st32", [32, 128], F32))
        xf2 = [es.enter_context(nc.sbuf_tensor(f"xf2_{i}", [128, T2], F32)) for i in range(1)]
        mrow = es.enter_context(nc.sbuf_tensor("mrow", [128, 2], F32))
        ssum = es.enter_context(nc.sbuf_tensor("ssum", [128, 2], F32))
        wstg_f = es.enter_context(nc.sbuf_tensor("wstg_f", [128, D], F32))
        wstg = es.enter_context(nc.sbuf_tensor("wstg", [128, D], BF16))
        constv = [LN_EPS, -0.5, SCALE]
        cst = es.enter_context(nc.sbuf_tensor("cst", [128, len(constv)], F32))
        for ci, cv in enumerate(constv):
            nc.const_aps.register(F32, cv, cst[:, ci:ci + 1])

        es_z = contextlib.ExitStack()
        z_sb = es_z.enter_context(nc.sbuf_tensor("z", [128, DC, T2], BF16))
        xb1 = es_z.enter_context(nc.sbuf_tensor("xb1", [128, T2], BF16))
        sq2 = es_z.enter_context(nc.sbuf_tensor("sq2", [128, T2], BF16))
        rbrow = es_z.enter_context(nc.sbuf_tensor("rbrow", [1, T2], F32))
        rrep = es_z.enter_context(nc.sbuf_tensor("rrep", [128, T2], F32))
        brep = es_z.enter_context(nc.sbuf_tensor("brep", [128, T2], F32))
        rt = [es_z.enter_context(nc.sbuf_tensor(f"rt{i}", [64, 1024], F32)) for i in range(2)]

        s_ld = Sem(es, nc, "s_ld")
        s_cast = Sem(es, nc, "s_cast")
        s_sq = Sem(es, nc, "s_sq")
        s_stat = Sem(es, nc, "s_stat")
        s_mm = Sem(es, nc, "s_mm")
        s_ev = Sem(es, nc, "s_ev")
        s_out = Sem(es, nc, "s_out")
        s_x3 = Sem(es, nc, "s_x3")
        s_init = Sem(es, nc, "s_init")
        s_vch = Sem(es, nc, "s_vch")
        s_sch = Sem(es, nc, "s_sch")
        s_wld = Sem(es, nc, "s_wld")
        s_sld = Sem(es, nc, "s_sld")
        s_idl = Sem(es, nc, "s_idl")
        block = es.enter_context(nc.Block())
        p = Plan()

        @p.on("vector")
        def _(v):
            v.memset(ones_k[:], 1.0)
            for ci, cv in enumerate(constv):
                v.memset(cst[:, ci:ci + 1], cv)
            v.memset(ones_r[:], 1.0).then_inc(s_init.h, 1)
        s_init.inc(1)

        @p.on("sync")
        def _(s):
            s.dma_start(out=idf[:], in_=IDF).then_inc(s_idl.h, 16)
            s.dma_start(out=idb[:], in_=IDB).then_inc(s_idl.h, 16)
            s.dma_start(out=cosb[:], in_=COS).then_inc(s_idl.h, 16)
            s.dma_start(out=sinb[:], in_=SIN).then_inc(s_idl.h, 16)
        id_ld = s_idl.inc(64)

        if upto == "noln":
            @p.on("vector")
            def _(v):
                v.wait_ge(s_init.h, 1)
                v.memset(xf2[0][:], 3.25).then_inc(s_stat.h, 1)

            @p.on("gpsimd")
            def _(g, base=s_out.n):
                g.wait_ge(s_stat.h, 1)
                g.dma_start(out=wstg_f[:], in_=WQ3[0].rearrange("(c p) n -> p c n", p=128)).then_inc(s_cast.h, 16)
                g.dma_start(out=wstg_f[:], in_=WK3[0].rearrange("(c p) n -> p c n", p=128)).then_inc(s_cast.h, 16)
                g.dma_start(out=wstg_f[:], in_=WV3[0].rearrange("(c p) n -> p c n", p=128)).then_inc(s_cast.h, 16)
                g.dma_start(out=wstg_f[:], in_=WO3[0].rearrange("(c p) n -> p c n", p=128)).then_inc(s_cast.h, 16)
                g.dma_start(out=wstg[:], in_=COS).then_inc(s_cast.h, 16)
                g.dma_start(out=wstg[:], in_=SIN).then_inc(s_cast.h, 16)
                g.dma_start(out=xf2[1][:], in_=XF[0:128, :]).then_inc(s_cast.h, 16)
                for dcq in range(DC):
                    g.dma_start(out=XO[dcq * 128:(dcq + 1) * 128, :],
                                in_=xf2[0][:, 0:TO]).then_inc(s_out.h, 16)
                g.wait_ge(s_out.h, base + 256)
            s_cast.inc(112)
            s_out.inc(256)
            es_z.close()
            p.run(block)
            return nc

        es_ln = contextlib.ExitStack()
        pss_s = es_ln.enter_context(nc.psum_tensor("pss_s", [128, 16], F32))
        pss_q = es_ln.enter_context(nc.psum_tensor("pss_q", [128, 16], F32))
        pr1 = es_ln.enter_context(nc.psum_tensor("pr1", [128, 512], F32))
        pr2 = es_ln.enter_context(nc.psum_tensor("pr2", [128, 512], F32))

        for dc in range(DC):
            w_buf = s_stat.n if dc >= 1 else None

            @p.on("gpsimd")
            def _(g, dc=dc, w=w_buf):
                if w is not None:
                    g.wait_ge(s_stat.h, w)
                g.dma_start(out=xb1[:], in_=XF[dc * 128:(dc + 1) * 128, :]).then_inc(s_cast.h, 16)
            ca = s_cast.inc(16)
            w_sq = s_stat.n if dc >= 1 else None

            @p.on("scalar")
            def _(s, ca=ca, w=w_sq):
                s.wait_ge(s_cast.h, ca)
                if w is not None:
                    s.wait_ge(s_stat.h, w)
                s.activation(sq2[:], xb1[:], AF.Square).then_inc(s_sq.h, 1)
            qa = s_sq.inc(1)

            @p.on("tensor")
            def _(t, dc=dc, qa=qa):
                t.wait_ge(s_sq.h, qa)
                if dc == 0:
                    t.wait_ge(s_init.h, 1)
                last = None
                for tc in range(16):
                    sl = slice(tc * 128, (tc + 1) * 128)
                    t.matmul(pss_s[:, tc:tc + 1], xb1[:, sl], ones_k[:],
                             start=(dc == 0 and tc == 0),
                             stop=(dc == DC - 1 and tc == 15))
                    last = t.matmul(pss_q[:, tc:tc + 1], sq2[:, sl], ones_k[:],
                                    start=(dc == 0 and tc == 0),
                                    stop=(dc == DC - 1 and tc == 15))
                last.then_inc(s_stat.h, 1)
            s_stat.inc(1)

        st_all = s_stat.n
        if upto == "ln0":
            @p.on("vector")
            def _(v, w=st_all):
                v.wait_ge(s_stat.h, w)
                v.tensor_copy(rb[:, 0:16], pss_s[:])
                v.tensor_copy(rb[:, 16:32], pss_q[:]).then_inc(s_stat.h, 1)
            lx = s_stat.inc(1)

            @p.on("gpsimd")
            def _(g, w=lx, base=s_out.n):
                g.wait_ge(s_stat.h, w)
                g.dma_start(out=wstg_f[:], in_=WQ3[0].rearrange("(c p) n -> p c n", p=128)).then_inc(s_cast.h, 16)
                g.dma_start(out=wstg_f[:], in_=WK3[0].rearrange("(c p) n -> p c n", p=128)).then_inc(s_cast.h, 16)
                g.dma_start(out=wstg_f[:], in_=WV3[0].rearrange("(c p) n -> p c n", p=128)).then_inc(s_cast.h, 16)
                g.dma_start(out=wstg_f[:], in_=WO3[0].rearrange("(c p) n -> p c n", p=128)).then_inc(s_cast.h, 16)
                g.dma_start(out=wstg[:], in_=COS).then_inc(s_cast.h, 16)
                g.dma_start(out=wstg[:], in_=SIN).then_inc(s_cast.h, 16)
                for dcq in range(DC):
                    g.dma_start(out=XO[dcq * 128:(dcq + 1) * 128, :],
                                in_=xf2[0][:, 0:TO] if dcq else rb[:, 0:48].broadcast_to([128, 48]) if False else xf2[0][:, 0:TO]).then_inc(s_out.h, 16)
                g.dma_start(out=XO[0:128, 0:48], in_=rb[:, 0:48]).then_inc(s_out.h, 16)
                g.wait_ge(s_out.h, base + 272)
            s_cast.inc(96)
            s_out.inc(272)
            es_z.close()
            p.run(block)
            return nc
        c0 = s_vch.n
        s_vch.inc(2)

        @p.on("vector")
        def _(v, w=st_all, c=c0):
            v.wait_ge(s_stat.h, w)
            v.tensor_scalar(rb[:, 0:16], pss_s[:], 1.0 / 2048.0, None, ALU.mult)
            v.tensor_scalar(rb[:, 32:48], pss_q[:], 1.0 / 2048.0, None, ALU.mult).then_inc(s_vch.h, 1)
            v.wait_ge(s_vch.h, c + 1)
            v.tensor_tensor(rb[:, 16:32], rb[:, 0:16], rb[:, 0:16], ALU.mult).then_inc(s_vch.h, 1)
            v.wait_ge(s_vch.h, c + 2)
            v.tensor_tensor(rb[:, 16:32], rb[:, 32:48], rb[:, 16:32], ALU.subtract).then_inc(s_stat.h, 1)
        l1 = s_stat.inc(1)
        sc0 = s_sch.n
        s_sch.inc(1)

        @p.on("scalar")
        def _(s, w=l1, c=sc0):
            s.wait_ge(s_stat.h, w)
            s.activation(rb[:, 32:48], rb[:, 16:32], AF.Ln, bias=LN_EPS).then_inc(s_sch.h, 1)
            s.wait_ge(s_sch.h, c + 1)
            s.activation(rb[:, 16:32], rb[:, 32:48], AF.Exp, scale=-0.5).then_inc(s_stat.h, 1)
        l2 = s_stat.inc(1)
        c1 = s_vch.n
        s_vch.inc(1)

        @p.on("vector")
        def _(v, w=l2, c=c1):
            v.wait_ge(s_stat.h, w)
            v.scalar_tensor_tensor(rb[:, 0:16], rb[:, 0:16], -1.0, rb[:, 16:32],
                                   ALU.mult, ALU.mult).then_inc(s_vch.h, 1)
            v.wait_ge(s_vch.h, c + 1)
            v.nop().then_inc(s_stat.h, 1)
        l3 = s_stat.inc(1)

        @p.on("tensor")
        def _(t, w=l3, idw=id_ld):
            t.wait_ge(s_stat.h, w)
            t.wait_ge(s_idl.h, idw)
            t.transpose(pr1[0:32, 0:128], rb[:, 0:32], idf[:]).then_inc(s_stat.h, 1)
        l4 = s_stat.inc(1)

        @p.on("vector")
        def _(v, w=l4):
            v.wait_ge(s_stat.h, w)
            v.tensor_copy(st32[:], pr1[0:32, 0:128]).then_inc(s_stat.h, 1)
        l5 = s_stat.inc(1)

        @p.on("scalar")
        def _(s, w=l5):
            s.wait_ge(s_stat.h, w)
            s.dma_start(out=SCR2.rearrange("(a c) -> a c", a=32), in_=st32[:]).then_inc(s_sld.h, 16)
        l6 = s_sld.inc(16)

        for which in range(2):
            @p.on("scalar")
            def _(s, which=which, w=l6, bc=(s_ev.n if which else None)):
                s.wait_ge(s_sld.h, w)
                if bc:
                    s.wait_ge(s_ev.h, bc)
                s.dma_start(out=rbrow[0:1, 0:T2],
                            in_=SCR2[which * 2048:(which + 1) * 2048]
                            .rearrange("(a c) -> a c", a=1)).then_inc(s_sld.h, 16)
            l7 = s_sld.inc(16)
            for tb in range(4):
                @p.on("tensor")
                def _(t, tb=tb, w=l7, ev=(s_ev.n if (tb or which) else None)):
                    t.wait_ge(s_sld.h, w)
                    if ev:
                        t.wait_ge(s_ev.h, ev)
                    t.matmul(pr1[:, 0:512], ones_r[:], rbrow[0:1, tb * 512:(tb + 1) * 512],
                             start=True, stop=True).then_inc(s_mm.h, 1)
                bm = s_mm.inc(1)

                @p.on("vector")
                def _(v, tb=tb, which=which, w=bm):
                    v.wait_ge(s_mm.h, w)
                    sl = slice(tb * 512, (tb + 1) * 512)
                    dst = brep if which == 0 else rrep
                    v.tensor_copy(dst[:, sl], pr1[:, 0:512]).then_inc(s_ev.h, 1)
                s_ev.inc(1)
        rep_done = s_ev.n
        if upto == "ln1":
            @p.on("gpsimd")
            def _(g, w=rep_done, base=s_out.n):
                g.wait_ge(s_ev.h, w)
                g.dma_start(out=wstg_f[:], in_=WQ3[0].rearrange("(c p) n -> p c n", p=128)).then_inc(s_cast.h, 16)
                g.dma_start(out=wstg_f[:], in_=WK3[0].rearrange("(c p) n -> p c n", p=128)).then_inc(s_cast.h, 16)
                g.dma_start(out=wstg_f[:], in_=WV3[0].rearrange("(c p) n -> p c n", p=128)).then_inc(s_cast.h, 16)
                g.dma_start(out=wstg_f[:], in_=WO3[0].rearrange("(c p) n -> p c n", p=128)).then_inc(s_cast.h, 16)
                g.dma_start(out=wstg[:], in_=COS).then_inc(s_cast.h, 16)
                g.dma_start(out=wstg[:], in_=SIN).then_inc(s_cast.h, 16)
                for dcq in range(DC):
                    g.dma_start(out=XO[dcq * 128:(dcq + 1) * 128, :],
                                in_=(rrep if dcq % 2 else brep)[:, 0:TO]).then_inc(s_out.h, 16)
                g.wait_ge(s_out.h, base + 256)
            s_cast.inc(96)
            s_out.inc(256)
            es_z.close()
            p.run(block)
            return nc

        for dc in range(DC):
            bi = 0
            w_buf = s_stat.n if dc >= 1 else None

            @p.on("sync")
            def _(s, dc=dc, bi=bi, w=w_buf):
                if w is not None:
                    s.wait_ge(s_stat.h, w)
                s.dma_start(out=xf2[bi][:], in_=XF[dc * 128:(dc + 1) * 128, :]).then_inc(s_ld.h, 16)
            la = s_ld.inc(16)
            zc = s_vch.n
            s_vch.inc(1)

            @p.on("vector")
            def _(v, dc=dc, bi=bi, la=la, c=zc, w5=(rep_done if dc == 0 else None),
                  kz=(upto == "z3")):
                v.wait_ge(s_ld.h, la)
                if w5 is not None:
                    v.wait_ge(s_ev.h, w5)
                if kz:
                    v.nop().then_inc(s_vch.h, 1)
                    v.nop().then_inc(s_stat.h, 1)
                    return
                v.tensor_tensor(xf2[bi][:], xf2[bi][:], rrep[:], ALU.mult).then_inc(s_vch.h, 1)
                v.wait_ge(s_vch.h, c + 1)
                v.tensor_tensor(z_sb[:, dc, :], xf2[bi][:], brep[:], ALU.add).then_inc(s_stat.h, 1)
            s_stat.inc(1)
        zf0 = s_stat.n
        if upto != "z5":
            zfc = s_vch.n
            s_vch.inc(1)

            @p.on("vector")
            def _(v, w=zf0, c=zfc):
                v.wait_ge(s_stat.h, w)
                v.memset(mrow[:, 0:1], 0.0).then_inc(s_vch.h, 1)
                v.wait_ge(s_vch.h, c + 1)
                v.nop().then_inc(s_stat.h, 1)
            z_done = s_stat.inc(1)
        else:
            z_done = zf0
        es_ln.close()
        if upto in ("z", "z2", "z3", "z5"):
            @p.on("gpsimd")
            def _(g, w=(zf0 if upto == "z5" else z_done), base=s_out.n):
                g.wait_ge(s_stat.h, w)
                g.dma_start(out=wstg_f[:], in_=WQ3[0].rearrange("(c p) n -> p c n", p=128)).then_inc(s_cast.h, 16)
                g.dma_start(out=wstg_f[:], in_=WK3[0].rearrange("(c p) n -> p c n", p=128)).then_inc(s_cast.h, 16)
                g.dma_start(out=wstg_f[:], in_=WV3[0].rearrange("(c p) n -> p c n", p=128)).then_inc(s_cast.h, 16)
                g.dma_start(out=wstg_f[:], in_=WO3[0].rearrange("(c p) n -> p c n", p=128)).then_inc(s_cast.h, 16)
                g.dma_start(out=wstg[:], in_=COS).then_inc(s_cast.h, 16)
                g.dma_start(out=wstg[:], in_=SIN).then_inc(s_cast.h, 16)
                for dcq in range(DC):
                    g.dma_start(out=XO[dcq * 128:(dcq + 1) * 128, :],
                                in_=(z_sb[:, dcq, 0:TO] if upto == "z" else xf2[0][:, 0:TO]))\
                        .then_inc(s_out.h, 16)
                g.wait_ge(s_out.h, base + 256)
            s_cast.inc(96)
            s_out.inc(256)
            es_z.close()
            p.run(block)
            return nc

        pbig = es.enter_context(nc.psum_tensor("pbig", [128, T2], F32))
        pq = es.enter_context(nc.psum_tensor("pq", [128, 1024], F32))
        ptp = es.enter_context(nc.psum_tensor("ptp", [128, 1024], BF16))

        def wload(W3, idx, dep_mm_val):
            @p.on("sync")
            def _(s, W3=W3, idx=idx, w=dep_mm_val):
                if w:
                    s.wait_ge(s_mm.h, w)
                s.dma_start(out=wstg_f[:], in_=W3[idx].rearrange("(c p) n -> p c n", p=128))\
                    .then_inc(s_ld.h, 16)
            la = s_ld.inc(16)

            @p.on("scalar")
            def _(s, la=la):
                s.wait_ge(s_ld.h, la)
                s.copy(wstg[:], wstg_f[:]).then_inc(s_wld.h, 1)
            return s_wld.inc(1)

        def emit_rope(src_ps, dst_fn, cs0, L, mm_wait):
            c2 = s_vch.n
            s_vch.inc(2)

            @p.on("vector")
            def _(v, src=src_ps, c=c2, cs0=cs0, L=L, dst_fn=dst_fn, mw=mm_wait):
                v.wait_ge(s_mm.h, mw)
                x1 = src[0:64, 0:L]
                x2 = src[64:128, 0:L]
                a1 = rt[0][0:64, 0:L]
                a2 = rt[1][0:64, 0:L]
                cs = slice(cs0, cs0 + L)
                v.tensor_tensor(a1, x1, cosb[0:64, cs], ALU.mult)
                v.tensor_tensor(a2, x2, sinb[0:64, cs], ALU.mult).then_inc(s_vch.h, 1)
                v.wait_ge(s_vch.h, c + 1)
                v.tensor_tensor(dst_fn(0), a1, a2, ALU.subtract)
                v.tensor_tensor(a1, x2, cosb[0:64, cs], ALU.mult)
                v.tensor_tensor(a2, x1, sinb[0:64, cs], ALU.mult).then_inc(s_vch.h, 1)
                v.wait_ge(s_vch.h, c + 2)
                v.tensor_tensor(dst_fn(1), a1, a2, ALU.add).then_inc(s_stat.h, 1)
            return s_stat.inc(1)

        for h in range(16):
            wl = wload(WQ3, h, s_mm.n)

            @p.on("tensor")
            def _(t, h=h, w=z_done, wl=wl, rp=s_stat.n):
                if h == 0:
                    t.wait_ge(s_stat.h, w)
                else:
                    t.wait_ge(s_stat.h, rp)
                t.wait_ge(s_wld.h, wl)
                last = None
                for kc in range(DC):
                    for tb in range(2):
                        sl = slice(tb * 512, (tb + 1) * 512)
                        last = t.matmul(pq[:, sl], wstg[:, kc * 128:(kc + 1) * 128],
                                        z_sb[:, kc, sl], start=(kc == 0), stop=(kc == DC - 1))
                last.then_inc(s_mm.h, 1)
            qm = s_mm.inc(1)
            emit_rope(pq, (lambda half, h=h: q_sb[64 * half:64 * half + 64, h, :]), 0, TO, qm)

        for kvh in range(4):
            wl = wload(WK3, kvh, s_mm.n)
            for half2 in range(2):
                @p.on("tensor")
                def _(t, kvh=kvh, half2=half2, rp=s_stat.n, wl=wl):
                    t.wait_ge(s_stat.h, rp)
                    t.wait_ge(s_wld.h, wl)
                    last = None
                    for kc in range(DC):
                        for tb in range(2):
                            sl_out = slice(tb * 512, (tb + 1) * 512)
                            sl_in = slice(half2 * 1024 + tb * 512, half2 * 1024 + (tb + 1) * 512)
                            last = t.matmul(pq[:, sl_out], wstg[:, kc * 128:(kc + 1) * 128],
                                            z_sb[:, kc, sl_in], start=(kc == 0), stop=(kc == DC - 1))
                    last.then_inc(s_mm.h, 1)
                km = s_mm.inc(1)
                emit_rope(pq, (lambda half, kvh=kvh, half2=half2:
                               k_sb[64 * half:64 * half + 64, kvh, half2 * 1024:(half2 + 1) * 1024]),
                          half2 * 1024, 1024, km)

        for kvh in range(4):
            wl = wload(WV3, kvh, s_mm.n)

            @p.on("tensor")
            def _(t, kvh=kvh, rp=s_stat.n, wl=wl, vq=(s_sq.n if kvh else None)):
                t.wait_ge(s_stat.h, rp)
                t.wait_ge(s_wld.h, wl)
                if vq:
                    t.wait_ge(s_sq.h, vq)
                last = None
                for kc in range(DC):
                    for tb in range(4):
                        sl = slice(tb * 512, (tb + 1) * 512)
                        last = t.matmul(pbig[:, sl], wstg[:, kc * 128:(kc + 1) * 128],
                                        z_sb[:, kc, sl], start=(kc == 0), stop=(kc == DC - 1))
                last.then_inc(s_mm.h, 1)
            vm = s_mm.inc(1)

            @p.on("scalar")
            def _(s, w=vm):
                s.wait_ge(s_mm.h, w)
                s.copy(vtmp[:], pbig[:, :]).then_inc(s_sq.h, 1)
            vq2 = s_sq.inc(1)

            for grp in range(2):
                @p.on("tensor")
                def _(t, grp=grp, w=vq2, ev=(s_ev.n if (kvh or grp) else None)):
                    t.wait_ge(s_sq.h, w)
                    if ev:
                        t.wait_ge(s_ev.h, ev)
                    last = None
                    for k in range(8):
                        lc = grp * 8 + k
                        last = t.transpose(ptp[:, k * 128:(k + 1) * 128],
                                           vtmp[:, lc * 128:(lc + 1) * 128], idb[:])
                    last.then_inc(s_mm.h, 1)
                tm = s_mm.inc(1)

                @p.on("vector")
                def _(v, kvh=kvh, grp=grp, w=tm):
                    v.wait_ge(s_mm.h, w)
                    last = None
                    for k in range(8):
                        lc = grp * 8 + k
                        last = v.tensor_copy(vT[:, kvh, lc, :], ptp[:, k * 128:(k + 1) * 128])
                    last.then_inc(s_ev.h, 1)
                s_ev.inc(1)
        v_done = s_ev.n
        if upto == "k2v":
            @p.on("gpsimd")
            def _(g, w=v_done, base=s_out.n):
                g.wait_ge(s_ev.h, w)
                g.dma_start(out=wstg_f[:], in_=WO3[0].rearrange("(c p) n -> p c n", p=128)).then_inc(s_cast.h, 16)
                for kh in range(4):
                    g.dma_start(out=XO[kh * 128:(kh + 1) * 128, :],
                                in_=k_sb[:, kh, 1024:2048]).then_inc(s_out.h, 16)
                for ch in range(8):
                    g.dma_start(out=XO[(4 + ch) * 128:(5 + ch) * 128, :],
                                in_=vT[:, :, :, :].rearrange("p a b n -> p (a b n)")[:, ch * 1024:(ch + 1) * 1024])\
                        .then_inc(s_out.h, 16)
                g.wait_ge(s_out.h, base + 192)
            s_cast.inc(16)
            s_out.inc(192)
            es_z.close()
            p.run(block)
            return nc
        if upto == "qkv":
            @p.on("gpsimd")
            def _(g, w=v_done, base=s_out.n):
                g.wait_ge(s_ev.h, w)
                for dcq in range(8):
                    g.dma_start(out=XO[dcq * 128:(dcq + 1) * 128, :],
                                in_=q_sb[:, dcq, :]).then_inc(s_out.h, 16)
                for dcq in range(4):
                    g.dma_start(out=XO[(8 + dcq) * 128:(9 + dcq) * 128, :],
                                in_=k_sb[:, dcq, 0:TO]).then_inc(s_out.h, 16)
                g.wait_ge(s_out.h, base + 192)
            s_out.inc(192)
            es_z.close()
            p.run(block)
            return nc
        es_z.close()
        e_h = es.enter_context(nc.sbuf_tensor("e_h", [128, 8, T2], BF16))
        eT = [es.enter_context(nc.sbuf_tensor(f"eT{i}", [128, 1024], BF16)) for i in range(2)]
        O_all = es.enter_context(nc.sbuf_tensor("O_all", [128, 16, TO], BF16))

        for h in range(16):
            kvh = h // 4
            for tq in range(8):
                sc_w = s_sq.n

                @p.on("tensor")
                def _(t, h=h, tq=tq, kvh=kvh, w=v_done, sw=sc_w):
                    if h == 0 and tq == 0:
                        t.wait_ge(s_ev.h, w)
                    t.wait_ge(s_sq.h, sw)
                    last = None
                    for tb in range(4):
                        sl = slice(tb * 512, (tb + 1) * 512)
                        last = t.matmul(pbig[:, sl], q_sb[:, h, tq * 128:(tq + 1) * 128],
                                        k_sb[:, kvh, sl], start=True, stop=True)
                    last.then_inc(s_mm.h, 1)
                sm = s_mm.inc(1)
                cc = s_vch.n
                s_vch.inc(1)

                @p.on("vector")
                def _(v, w=sm, c=cc):
                    v.wait_ge(s_mm.h, w)
                    v.tensor_reduce(mrow[:, 0:1], pbig[:, :], AX.X, ALU.max).then_inc(s_vch.h, 1)
                    v.wait_ge(s_vch.h, c + 1)
                    v.tensor_scalar(mrow[:, 1:2], mrow[:, 0:1], -SCALE, None, ALU.mult)\
                        .then_inc(s_stat.h, 1)
                mx = s_stat.inc(1)

                @p.on("scalar")
                def _(s, h=h, tq=tq, w=mx):
                    s.wait_ge(s_stat.h, w)
                    s.activation(e_h[:, tq, :], pbig[:, :], AF.Exp, bias=mrow[:, 1:2],
                                 scale=SCALE, accum_out=ssum[:, 0:1]).then_inc(s_sq.h, 1)
                ex = s_sq.inc(1)
                cc2 = s_vch.n
                s_vch.inc(2)

                @p.on("vector")
                def _(v, h=h, tq=tq, w=ex, c=cc2):
                    v.wait_ge(s_sq.h, w)
                    v.reciprocal(ssum[:, 1:2], ssum[:, 0:1]).then_inc(s_vch.h, 1)
                    v.wait_ge(s_vch.h, c + 1)
                    v.tensor_scalar(e_h[:, tq, :], e_h[:, tq, :], ssum[:, 1:2], None, ALU.mult)\
                        .then_inc(s_vch.h, 1)
                    v.wait_ge(s_vch.h, c + 2)
                    v.nop().then_inc(s_stat.h, 1)
                s_stat.inc(1)
            e_done = s_stat.n
            if upto == "e0" and h == 0:
                @p.on("gpsimd")
                def _(g, w=e_done, base=s_out.n):
                    g.wait_ge(s_stat.h, w)
                    g.dma_start(out=wstg_f[:], in_=WO3[0].rearrange("(c p) n -> p c n", p=128)).then_inc(s_cast.h, 16)
                    for tq in range(8):
                        for hf in range(2):
                            g.dma_start(out=XO[(tq * 2 + hf) * 128:(tq * 2 + hf + 1) * 128, :],
                                        in_=e_h[:, tq, hf * 1024:(hf + 1) * 1024]).then_inc(s_out.h, 16)
                    g.wait_ge(s_out.h, base + 256)
                s_cast.inc(16)
                s_out.inc(256)
                p.run(block)
                return nc

            av_of = {}
            for lc in range(16):
                bi = lc % 2

                @p.on("tensor")
                def _(t, h=h, lc=lc, w=e_done, ev=s_ev.n):
                    if lc == 0:
                        t.wait_ge(s_stat.h, w)
                    if ev:
                        t.wait_ge(s_ev.h, ev)
                    last = None
                    for tq in range(8):
                        last = t.transpose(ptp[:, tq * 128:(tq + 1) * 128],
                                           e_h[:, tq, lc * 128:(lc + 1) * 128], idb[:])
                    last.then_inc(s_mm.h, 1)
                tm = s_mm.inc(1)

                @p.on("scalar")
                def _(s, lc=lc, bi=bi, w=tm,
                      fw=(av_of[lc - 2] if lc >= 2 else None)):
                    s.wait_ge(s_mm.h, w)
                    if fw is not None:
                        s.wait_ge(s_mm.h, fw)
                    s.copy(eT[bi][:], ptp[:]).then_inc(s_ev.h, 1)
                te = s_ev.inc(1)

                @p.on("tensor")
                def _(t, h=h, lc=lc, bi=bi, kvh=kvh, w=te, ow=(s_sq.n if (lc == 0 and h > 0) else None)):
                    t.wait_ge(s_ev.h, w)
                    if ow is not None:
                        t.wait_ge(s_sq.h, ow)
                    last = None
                    for tb in range(2):
                        sl = slice(tb * 512, (tb + 1) * 512)
                        last = t.matmul(pq[:, sl], vT[:, kvh, lc, :], eT[bi][:, sl],
                                        start=(lc == 0), stop=(lc == 15))
                    last.then_inc(s_mm.h, 1)
                av_of[lc] = s_mm.inc(1)
            av = s_mm.n

            @p.on("scalar")
            def _(s, h=h, w=av):
                s.wait_ge(s_mm.h, w)
                s.copy(O_all[:, h, :], pq[:, :]).then_inc(s_sq.h, 1)
            s_sq.inc(1)
        o_done = s_sq.n
        if upto == "ov":
            @p.on("gpsimd")
            def _(g, w=o_done, base=s_out.n):
                g.wait_ge(s_sq.h, w)
                g.dma_start(out=wstg_f[:], in_=WO3[0].rearrange("(c p) n -> p c n", p=128)).then_inc(s_cast.h, 16)
                for hh in range(16):
                    g.dma_start(out=XO[hh * 128:(hh + 1) * 128, :],
                                in_=O_all[:, hh, :]).then_inc(s_out.h, 16)
                g.wait_ge(s_out.h, base + 256)
            s_cast.inc(16)
            s_out.inc(256)
            p.run(block)
            return nc

        for j in range(DC):
            bi = 0
            wl = wload(WO3, j, s_mm.n)

            @p.on("tensor")
            def _(t, j=j, w=o_done, wl=wl, aw=(s_ev.n if j >= 1 else None)):
                if j == 0:
                    t.wait_ge(s_sq.h, w)
                t.wait_ge(s_wld.h, wl)
                if aw:
                    t.wait_ge(s_ev.h, aw)
                last = None
                for tb in range(2):
                    sl = slice(tb * 512, (tb + 1) * 512)
                    for kc in range(DC):
                        last = t.matmul(pbig[:, sl], wstg[:, kc * 128:(kc + 1) * 128],
                                        O_all[:, kc, sl], start=(kc == 0), stop=(kc == DC - 1))
                last.then_inc(s_mm.h, 1)
            om = s_mm.inc(1)

            @p.on("sync")
            def _(s, j=j, bi=bi, w=(s_out.n if j >= 1 else None)):
                if w:
                    s.wait_ge(s_out.h, w)
                s.dma_start(out=xf2[bi][:, 0:TO], in_=XF[j * 128:(j + 1) * 128, 0:TO])\
                    .then_inc(s_x3.h, 16)
            xa = s_x3.inc(16)

            @p.on("vector")
            def _(v, j=j, bi=bi, w=om, xa=xa):
                v.wait_ge(s_mm.h, w)
                v.wait_ge(s_x3.h, xa)
                v.tensor_tensor(xf2[bi][:, 0:TO], xf2[bi][:, 0:TO], pbig[:, 0:TO], ALU.add)\
                    .then_inc(s_ev.h, 1)
            za = s_ev.inc(1)

            @p.on("scalar")
            def _(s, j=j, bi=bi, za=za):
                s.wait_ge(s_ev.h, za)
                s.dma_start(out=XO[j * 128:(j + 1) * 128, :], in_=xf2[bi][:, 0:TO])\
                    .then_inc(s_out.h, 16)
            s_out.inc(16)

        @p.on("scalar")
        def _(s, n=s_out.n):
            s.wait_ge(s_out.h, n)

        p.run(block)
    return nc


def build_mamba():
    nc = bass.Bass("TRN2", target_bir_lowering=False)
    T2 = 2048
    DC = 16
    XF = nc.dram_tensor("xf", [D, T2], F32, kind="ExternalInput").ap()
    XPWD = nc.dram_tensor("xpwd", [128, 2048], F32, kind="ExternalInput").ap()
    XPWBC = nc.dram_tensor("xpwbc", [128, 512], F32, kind="ExternalInput").ap()
    DTPT = nc.dram_tensor("dtpt", [128, 1024], F32, kind="ExternalInput").ap()
    DTB = nc.dram_tensor("dtb", [128, 8], F32, kind="ExternalInput").ap()
    ALOGT = nc.dram_tensor("alogt", [128, 128], F32, kind="ExternalInput").ap()
    DVIN = nc.dram_tensor("dvin", [128, 8], F32, kind="ExternalInput").ap()
    PATS = nc.dram_tensor("pats", [128, 2048], BF16, kind="ExternalInput").ap()
    PATBC = nc.dram_tensor("patbc", [32, 256], BF16, kind="ExternalInput").ap()
    PAT8 = nc.dram_tensor("pat8", [128, 8], BF16, kind="ExternalInput").ap()
    IDF = nc.dram_tensor("idf", [128, 128], F32, kind="ExternalInput").ap()
    HHC = nc.dram_tensor("hhc", [1], F32, kind="ExternalInput").ap()
    XO = nc.dram_tensor("xo", [1024, T2], F32, kind="ExternalOutput").ap()
    SCR2 = nc.dram_tensor("scr2", [4096], F32, kind="Internal").ap()

    es = contextlib.ExitStack()
    with es:
        z_sb = es.enter_context(nc.sbuf_tensor("z", [128, DC, T2], BF16))
        xpwd_b = es.enter_context(nc.sbuf_tensor("xpwd_b", [128, 2048], BF16))
        xpwbc_b = es.enter_context(nc.sbuf_tensor("xpwbc_b", [128, 512], BF16))
        dtpt_b = es.enter_context(nc.sbuf_tensor("dtpt_b", [128, 1024], BF16))
        pats_sb = es.enter_context(nc.sbuf_tensor("pats_s", [128, 2048], BF16))
        patbc_sb = es.enter_context(nc.sbuf_tensor("patbc_s", [32, 256], BF16))
        pat8_sb = es.enter_context(nc.sbuf_tensor("pat8_s", [128, 8], BF16))
        dtb_sb = es.enter_context(nc.sbuf_tensor("dtb_s", [128, 8], F32))
        alog_sb = es.enter_context(nc.sbuf_tensor("alog_s", [128, 128], F32))
        at_sb = es.enter_context(nc.sbuf_tensor("at_s", [128, 128], F32))
        dv_sb = es.enter_context(nc.sbuf_tensor("dv_s", [128, 8], F32))
        hh_sb = es.enter_context(nc.sbuf_tensor("hh_s", [1, 1], F32))
        dtlow_b = es.enter_context(nc.sbuf_tensor("dtlow_b", [128, T2], BF16))
        bc_b = es.enter_context(nc.sbuf_tensor("bc_b", [32, T2], BF16))
        brep_m = es.enter_context(nc.sbuf_tensor("brep_m", [128, T2], BF16))
        crep_m = es.enter_context(nc.sbuf_tensor("crep_m", [128, T2], BF16))
        dt_b = es.enter_context(nc.sbuf_tensor("dt_b", [128, T2], BF16))
        dtz_b = es.enter_context(nc.sbuf_tensor("dtz_b", [128, T2], BF16))
        idf = es.enter_context(nc.sbuf_tensor("idf_s", [128, 128], F32))
        ones_k = es.enter_context(nc.sbuf_tensor("ones_k", [128, 1], BF16))
        ones_r = es.enter_context(nc.sbuf_tensor("ones_r", [1, 128], F32))
        rb = es.enter_context(nc.sbuf_tensor("rb", [128, 48], F32))
        st32 = es.enter_context(nc.sbuf_tensor("st32", [32, 128], F32))
        constv = [LN_EPS, -0.5, -1.0, 1.0]
        cst = es.enter_context(nc.sbuf_tensor("cst", [128, len(constv)], F32))
        for ci, cv in enumerate(constv):
            nc.const_aps.register(F32, cv, cst[:, ci:ci + 1])

        es_ln = contextlib.ExitStack()
        xb1 = es_ln.enter_context(nc.sbuf_tensor("xb1", [128, T2], BF16))
        sq2 = es_ln.enter_context(nc.sbuf_tensor("sq2", [128, T2], BF16))
        xf2 = es_ln.enter_context(nc.sbuf_tensor("xf2", [128, T2], F32))
        rbrow = es_ln.enter_context(nc.sbuf_tensor("rbrow", [1, T2], F32))
        rrep = es_ln.enter_context(nc.sbuf_tensor("rrep", [128, T2], F32))
        brep = es_ln.enter_context(nc.sbuf_tensor("brep", [128, T2], F32))

        s_ld = Sem(es, nc, "s_ld")
        s_cast = Sem(es, nc, "s_cast")
        s_sq = Sem(es, nc, "s_sq")
        s_stat = Sem(es, nc, "s_stat")
        s_mm = Sem(es, nc, "s_mm")
        s_ev = Sem(es, nc, "s_ev")
        s_out = Sem(es, nc, "s_out")
        s_x3 = Sem(es, nc, "s_x3")
        s_init = Sem(es, nc, "s_init")
        s_vch = Sem(es, nc, "s_vch")
        s_sch = Sem(es, nc, "s_sch")
        s_sld = Sem(es, nc, "s_sld")
        s_idl = Sem(es, nc, "s_idl")
        block = es.enter_context(nc.Block())
        p = Plan()

        @p.on("vector")
        def _(v):
            v.memset(ones_k[:], 1.0)
            for ci, cv in enumerate(constv):
                v.memset(cst[:, ci:ci + 1], cv)
            v.memset(ones_r[:], 1.0).then_inc(s_init.h, 1)
        s_init.inc(1)

        @p.on("sync")
        def _(s):
            s.dma_start(out=idf[:], in_=IDF).then_inc(s_idl.h, 16)
            s.dma_start(out=pats_sb[:], in_=PATS).then_inc(s_idl.h, 16)
            s.dma_start(out=patbc_sb[:], in_=PATBC).then_inc(s_idl.h, 16)
            s.dma_start(out=pat8_sb[:], in_=PAT8).then_inc(s_idl.h, 16)
            s.dma_start(out=dtb_sb[:], in_=DTB).then_inc(s_idl.h, 16)
            s.dma_start(out=alog_sb[:], in_=ALOGT).then_inc(s_idl.h, 16)
            s.dma_start(out=dv_sb[:], in_=DVIN).then_inc(s_idl.h, 16)
            s.dma_start(out=hh_sb[:], in_=HHC.rearrange("(a c) -> a c", a=1)).then_inc(s_idl.h, 16)
        id_ld = s_idl.inc(128)

        @p.on("gpsimd")
        def _(g):
            g.dma_start(out=xpwd_b[:], in_=XPWD).then_inc(s_cast.h, 16)
            g.dma_start(out=xpwbc_b[:], in_=XPWBC).then_inc(s_cast.h, 16)
            g.dma_start(out=dtpt_b[:], in_=DTPT).then_inc(s_cast.h, 16)
        w_cast = s_cast.inc(48)

        @p.on("scalar")
        def _(s, w=id_ld):
            s.wait_ge(s_idl.h, w)
            s.activation(at_sb[:], alog_sb[:], AF.Exp).then_inc(s_sq.h, 1)
        at1 = s_sq.inc(1)

        @p.on("vector")
        def _(v, w=at1):
            v.wait_ge(s_sq.h, w)
            v.tensor_scalar(at_sb[:], at_sb[:], -1.0, None, ALU.mult).then_inc(s_ev.h, 1)
        at_done = s_ev.inc(1)

        es_lnp = contextlib.ExitStack()
        pss_s = es_lnp.enter_context(nc.psum_tensor("pss_s", [128, 16], F32))
        pss_q = es_lnp.enter_context(nc.psum_tensor("pss_q", [128, 16], F32))
        pr1 = es_lnp.enter_context(nc.psum_tensor("pr1", [128, 512], F32))

        for dc in range(DC):
            w_buf = s_stat.n if dc >= 1 else None

            @p.on("gpsimd")
            def _(g, dc=dc, w=w_buf):
                if w is not None:
                    g.wait_ge(s_stat.h, w)
                g.dma_start(out=xb1[:], in_=XF[dc * 128:(dc + 1) * 128, :]).then_inc(s_cast.h, 16)
            ca = s_cast.inc(16)
            w_sq = s_stat.n if dc >= 1 else None

            @p.on("scalar")
            def _(s, ca=ca, w=w_sq):
                s.wait_ge(s_cast.h, ca)
                if w is not None:
                    s.wait_ge(s_stat.h, w)
                s.activation(sq2[:], xb1[:], AF.Square).then_inc(s_sq.h, 1)
            qa = s_sq.inc(1)

            @p.on("tensor")
            def _(t, dc=dc, qa=qa):
                t.wait_ge(s_sq.h, qa)
                if dc == 0:
                    t.wait_ge(s_init.h, 1)
                last = None
                for tc in range(16):
                    sl = slice(tc * 128, (tc + 1) * 128)
                    t.matmul(pss_s[:, tc:tc + 1], xb1[:, sl], ones_k[:],
                             start=(dc == 0 and tc == 0),
                             stop=(dc == DC - 1 and tc == 15))
                    last = t.matmul(pss_q[:, tc:tc + 1], sq2[:, sl], ones_k[:],
                                    start=(dc == 0 and tc == 0),
                                    stop=(dc == DC - 1 and tc == 15))
                last.then_inc(s_stat.h, 1)
            s_stat.inc(1)
        st_all = s_stat.n
        c0 = s_vch.n
        s_vch.inc(2)

        @p.on("vector")
        def _(v, w=st_all, c=c0):
            v.wait_ge(s_stat.h, w)
            v.tensor_scalar(rb[:, 0:16], pss_s[:], 1.0 / 2048.0, None, ALU.mult)
            v.tensor_scalar(rb[:, 32:48], pss_q[:], 1.0 / 2048.0, None, ALU.mult).then_inc(s_vch.h, 1)
            v.wait_ge(s_vch.h, c + 1)
            v.tensor_tensor(rb[:, 16:32], rb[:, 0:16], rb[:, 0:16], ALU.mult).then_inc(s_vch.h, 1)
            v.wait_ge(s_vch.h, c + 2)
            v.tensor_tensor(rb[:, 16:32], rb[:, 32:48], rb[:, 16:32], ALU.subtract).then_inc(s_stat.h, 1)
        l1 = s_stat.inc(1)
        sc0 = s_sch.n
        s_sch.inc(1)

        @p.on("scalar")
        def _(s, w=l1, c=sc0):
            s.wait_ge(s_stat.h, w)
            s.activation(rb[:, 32:48], rb[:, 16:32], AF.Ln, bias=LN_EPS).then_inc(s_sch.h, 1)
            s.wait_ge(s_sch.h, c + 1)
            s.activation(rb[:, 16:32], rb[:, 32:48], AF.Exp, scale=-0.5).then_inc(s_stat.h, 1)
        l2 = s_stat.inc(1)
        c1 = s_vch.n
        s_vch.inc(1)

        @p.on("vector")
        def _(v, w=l2, c=c1):
            v.wait_ge(s_stat.h, w)
            v.scalar_tensor_tensor(rb[:, 0:16], rb[:, 0:16], -1.0, rb[:, 16:32],
                                   ALU.mult, ALU.mult).then_inc(s_vch.h, 1)
            v.wait_ge(s_vch.h, c + 1)
            v.nop().then_inc(s_stat.h, 1)
        l3 = s_stat.inc(1)

        @p.on("tensor")
        def _(t, w=l3, idw=id_ld):
            t.wait_ge(s_stat.h, w)
            t.wait_ge(s_idl.h, idw)
            t.transpose(pr1[0:32, 0:128], rb[:, 0:32], idf[:]).then_inc(s_stat.h, 1)
        l4 = s_stat.inc(1)

        @p.on("vector")
        def _(v, w=l4):
            v.wait_ge(s_stat.h, w)
            v.tensor_copy(st32[:], pr1[0:32, 0:128]).then_inc(s_stat.h, 1)
        l5 = s_stat.inc(1)

        @p.on("scalar")
        def _(s, w=l5):
            s.wait_ge(s_stat.h, w)
            s.dma_start(out=SCR2.rearrange("(a c) -> a c", a=32), in_=st32[:]).then_inc(s_sld.h, 16)
        l6 = s_sld.inc(16)

        for which in range(2):
            @p.on("scalar")
            def _(s, which=which, w=l6, bc=(s_ev.n if which else None)):
                s.wait_ge(s_sld.h, w)
                if bc:
                    s.wait_ge(s_ev.h, bc)
                s.dma_start(out=rbrow[0:1, 0:T2],
                            in_=SCR2[which * 2048:(which + 1) * 2048]
                            .rearrange("(a c) -> a c", a=1)).then_inc(s_sld.h, 16)
            l7 = s_sld.inc(16)
            for tb in range(4):
                @p.on("tensor")
                def _(t, tb=tb, w=l7, ev=(s_ev.n if (tb or which) else None)):
                    t.wait_ge(s_sld.h, w)
                    if ev:
                        t.wait_ge(s_ev.h, ev)
                    t.matmul(pr1[:, 0:512], ones_r[:], rbrow[0:1, tb * 512:(tb + 1) * 512],
                             start=True, stop=True).then_inc(s_mm.h, 1)
                bm = s_mm.inc(1)

                @p.on("vector")
                def _(v, tb=tb, which=which, w=bm):
                    v.wait_ge(s_mm.h, w)
                    sl = slice(tb * 512, (tb + 1) * 512)
                    dst = brep if which == 0 else rrep
                    v.tensor_copy(dst[:, sl], pr1[:, 0:512]).then_inc(s_ev.h, 1)
                s_ev.inc(1)
        rep_done = s_ev.n

        for dc in range(DC):
            w_buf = s_stat.n if dc >= 1 else None

            @p.on("sync")
            def _(s, dc=dc, w=w_buf):
                if w is not None:
                    s.wait_ge(s_stat.h, w)
                s.dma_start(out=xf2[:], in_=XF[dc * 128:(dc + 1) * 128, :]).then_inc(s_ld.h, 16)
            la = s_ld.inc(16)
            zc = s_vch.n
            s_vch.inc(1)

            @p.on("vector")
            def _(v, dc=dc, la=la, c=zc, w5=(rep_done if dc == 0 else None)):
                v.wait_ge(s_ld.h, la)
                if w5 is not None:
                    v.wait_ge(s_ev.h, w5)
                v.tensor_tensor(xf2[:], xf2[:], rrep[:], ALU.mult).then_inc(s_vch.h, 1)
                v.wait_ge(s_vch.h, c + 1)
                v.tensor_tensor(z_sb[:, dc, :], xf2[:], brep[:], ALU.add).then_inc(s_stat.h, 1)
            s_stat.inc(1)
        z_done = s_stat.n
        es_ln.close()
        es_lnp.close()

        sA = es.enter_context(nc.sbuf_tensor("sA", [128, T2], F32))
        sB = es.enter_context(nc.sbuf_tensor("sB", [128, T2], F32))
        sC = es.enter_context(nc.sbuf_tensor("sC", [128, T2], F32))
        a_sb = es.enter_context(nc.sbuf_tensor("a_sb", [128, T2], F32))
        u_sb = es.enter_context(nc.sbuf_tensor("u_sb", [128, T2], F32))
        h_sb = es.enter_context(nc.sbuf_tensor("h_sb", [128, T2], F32))
        prod_b = es.enter_context(nc.sbuf_tensor("prod_b", [128, T2], BF16))
        y_dt = es.enter_context(nc.sbuf_tensor("y_dt", [128, T2], F32))
        y8 = es.enter_context(nc.sbuf_tensor("y8", [8, T2], F32))
        xown = es.enter_context(nc.sbuf_tensor("xown", [128, T2], F32))

        p_main = es.enter_context(nc.psum_tensor("p_main", [128, T2], F32))
        py_big = es.enter_context(nc.psum_tensor("py_big", [128, T2], F32))

        @p.on("tensor")
        def _(t, w=z_done, wc=w_cast):
            t.wait_ge(s_stat.h, w)
            t.wait_ge(s_cast.h, wc)
            last = None
            for kc in range(DC):
                for nb in range(4):
                    sl = slice(nb * 512, (nb + 1) * 512)
                    last = t.matmul(p_main[:, sl], xpwd_b[:, kc * 128:(kc + 1) * 128],
                                    z_sb[:, kc, sl], start=(kc == 0), stop=(kc == DC - 1))
            last.then_inc(s_mm.h, 1)
        dtl_mm = s_mm.inc(1)

        @p.on("scalar")
        def _(s, w=dtl_mm):
            s.wait_ge(s_mm.h, w)
            s.copy(dtlow_b[:], p_main[:, :]).then_inc(s_sq.h, 1)
        dtl_ev = s_sq.inc(1)

        @p.on("tensor")
        def _(t):
            last = None
            for kc in range(DC):
                for nb in range(4):
                    sl = slice(nb * 512, (nb + 1) * 512)
                    last = t.matmul(py_big[0:32, sl], xpwbc_b[:, kc * 32:(kc + 1) * 32],
                                    z_sb[:, kc, sl], start=(kc == 0), stop=(kc == DC - 1))
            last.then_inc(s_mm.h, 1)
        bc_mm = s_mm.inc(1)

        @p.on("scalar")
        def _(s, w=bc_mm):
            s.wait_ge(s_mm.h, w)
            s.copy(bc_b[:], py_big[0:32, :]).then_inc(s_sq.h, 1)
        bc_ev = s_sq.inc(1)

        for j in range(2):
            @p.on("tensor")
            def _(t, j=j, w=bc_ev, de=dtl_ev, ev=(s_ev.n if j else None)):
                t.wait_ge(s_sq.h, w)
                t.wait_ge(s_sq.h, de)
                if ev:
                    t.wait_ge(s_ev.h, ev)
                last = None
                for nb in range(4):
                    sl = slice(nb * 512, (nb + 1) * 512)
                    last = t.matmul(p_main[:, sl], patbc_sb[:, j * 128:(j + 1) * 128],
                                    bc_b[:, sl], start=True, stop=True)
                last.then_inc(s_mm.h, 1)
            rj = s_mm.inc(1)

            @p.on("vector")
            def _(v, j=j, w=rj):
                v.wait_ge(s_mm.h, w)
                dst = brep_m if j == 0 else crep_m
                v.tensor_copy(dst[:], p_main[:, :]).then_inc(s_ev.h, 1)
            s_ev.inc(1)
        reps_done = s_ev.n

        re_prev = None
        for dtile in range(8):
            zcn = dtile
            @p.on("tensor")
            def _(t, dtile=dtile, w=reps_done, re=re_prev, ye=s_sq.n, ow=s_out.n):
                if dtile == 0:
                    t.wait_ge(s_ev.h, w)
                else:
                    t.wait_ge(s_sq.h, ye)
                    t.wait_ge(s_ev.h, re)
                    t.wait_ge(s_out.h, ow)
                last = None
                for nb in range(4):
                    sl = slice(nb * 512, (nb + 1) * 512)
                    last = t.matmul(p_main[:, sl], dtpt_b[:, dtile * 128:(dtile + 1) * 128],
                                    dtlow_b[:, sl], start=True, stop=True)
                last.then_inc(s_mm.h, 1)
            pdt_mm = s_mm.inc(1)

            vc = s_vch.n
            s_vch.inc(3)

            @p.on("vector")
            def _(v, dtile=dtile, w=pdt_mm, c=vc):
                v.wait_ge(s_mm.h, w)
                v.tensor_scalar(sA[:], p_main[:, :], dtb_sb[:, dtile:dtile + 1], None,
                                ALU.add).then_inc(s_vch.h, 1)
                v.wait_ge(s_vch.h, c + 1)
                v.tensor_scalar(sB[:], sA[:], -1.0, None, ALU.mult).then_inc(s_vch.h, 1)
                v.wait_ge(s_vch.h, c + 2)
                v.tensor_tensor(sB[:], sA[:], sB[:], ALU.max).then_inc(s_vch.h, 1)
            ab_done = s_vch.n

            @p.on("vector")
            def _(v, c=ab_done):
                v.wait_ge(s_vch.h, c)
                v.nop().then_inc(s_stat.h, 1)
            ab_st = s_stat.inc(1)

            sc = s_sch.n
            s_sch.inc(1)

            @p.on("scalar")
            def _(s, w=ab_st, c=sc):
                s.wait_ge(s_stat.h, w)
                s.activation(sC[:], sB[:], AF.Exp, scale=-1.0).then_inc(s_sch.h, 1)
                s.wait_ge(s_sch.h, c + 1)
                s.activation(sC[:], sC[:], AF.Ln, bias=1.0).then_inc(s_sq.h, 1)
            ln_done = s_sq.inc(1)

            vc2 = s_vch.n
            s_vch.inc(2)

            @p.on("vector")
            def _(v, dtile=dtile, w=ln_done, c=vc2, zcn=zcn):
                v.wait_ge(s_sq.h, w)
                v.tensor_scalar(sB[:], sA[:], 0.0, None, ALU.max).then_inc(s_vch.h, 1)
                v.wait_ge(s_vch.h, c + 1)
                v.tensor_tensor(dt_b[:], sB[:], sC[:], ALU.add).then_inc(s_vch.h, 1)
                v.wait_ge(s_vch.h, c + 2)
                v.tensor_tensor(dtz_b[:], dt_b[:], z_sb[:, zcn, :], ALU.mult)\
                    .then_inc(s_stat.h, 1)
            dt_done = s_stat.inc(1)

            ye_last = None
            for st16 in range(16):
                st = dtile * 16 + st16
                ye_prev = s_sq.n

                @p.on("tensor")
                def _(t, st16=st16, w=dt_done, yp=(ye_prev if st16 else None),
                      ad=at_done):
                    if st16 == 0:
                        t.wait_ge(s_stat.h, w)
                        t.wait_ge(s_ev.h, ad)
                    else:
                        t.wait_ge(s_sq.h, yp)
                    last = None
                    for nb in range(4):
                        sl = slice(nb * 512, (nb + 1) * 512)
                        last = t.matmul(p_main[:, sl], pats_sb[:, st16 * 128:(st16 + 1) * 128],
                                        dt_b[:, sl], start=True, stop=True)
                    last.then_inc(s_mm.h, 1)
                ra_mm = s_mm.inc(1)

                @p.on("scalar")
                def _(s, st=st, w=ra_mm):
                    s.wait_ge(s_mm.h, w)
                    s.activation(a_sb[:], p_main[:, :], AF.Exp,
                                 scale=at_sb[:, st:st + 1]).then_inc(s_sq.h, 1)
                a_done = s_sq.inc(1)

                @p.on("tensor")
                def _(t, st16=st16, w=dt_done, ue=(s_ev.n if (st16 or dtile) else None)):
                    if st16 == 0:
                        t.wait_ge(s_stat.h, w)
                    if ue:
                        t.wait_ge(s_ev.h, ue)
                    last = None
                    for nb in range(4):
                        sl = slice(nb * 512, (nb + 1) * 512)
                        last = t.matmul(py_big[:, sl], pats_sb[:, st16 * 128:(st16 + 1) * 128],
                                        dtz_b[:, sl], start=True, stop=True)
                    last.then_inc(s_mm.h, 1)
                rz_mm = s_mm.inc(1)

                vc3 = s_vch.n
                s_vch.inc(3)

                @p.on("vector")
                def _(v, w=rz_mm, aw=a_done, c=vc3):
                    v.wait_ge(s_mm.h, w)
                    v.tensor_tensor(u_sb[:], py_big[:, :], brep_m[:], ALU.mult)\
                        .then_inc(s_vch.h, 1)
                    v.wait_ge(s_vch.h, c + 1)
                    v.wait_ge(s_sq.h, aw)
                    v.tensor_tensor_scan(h_sb[:], a_sb[:], u_sb[:], 0.0,
                                         ALU.mult, ALU.add).then_inc(s_vch.h, 1)
                    v.wait_ge(s_vch.h, c + 2)
                    v.tensor_tensor(prod_b[:], h_sb[:], crep_m[:], ALU.mult)\
                        .then_inc(s_vch.h, 1)
                pr_done = s_vch.n

                @p.on("vector")
                def _(v, c=pr_done):
                    v.wait_ge(s_vch.h, c)
                    v.nop().then_inc(s_ev.h, 1)
                pr_ev = s_ev.inc(1)

                @p.on("tensor")
                def _(t, w=pr_ev, aw=a_done):
                    t.wait_ge(s_ev.h, w)
                    t.wait_ge(s_sq.h, aw)
                    last = None
                    for nb in range(4):
                        sl = slice(nb * 512, (nb + 1) * 512)
                        last = t.matmul(p_main[0:8, sl], pat8_sb[:],
                                        prod_b[:, sl], start=True, stop=True)
                    last.then_inc(s_mm.h, 1)
                y_mm = s_mm.inc(1)

                ye_mid = s_sq.n + 1

                @p.on("scalar")
                def _(s, st16=st16, w=y_mm, m=ye_mid):
                    s.wait_ge(s_mm.h, w)
                    s.copy(y8[:], p_main[0:8, :]).then_inc(s_sq.h, 1)
                    s.wait_ge(s_sq.h, m)
                    s.dma_start(out=y_dt[st16 * 8:(st16 + 1) * 8, :], in_=y8[:])\
                        .then_inc(s_sq.h, 16)
                s_sq.inc(1)
                ye_last = s_sq.inc(16)

            @p.on("sync")
            def _(s, dtile=dtile, re=re_prev):
                if re:
                    s.wait_ge(s_ev.h, re)
                s.dma_start(out=xown[:], in_=XF[dtile * 128:(dtile + 1) * 128, :])\
                    .then_inc(s_x3.h, 16)
            xa = s_x3.inc(16)

            vc4 = s_vch.n
            s_vch.inc(2)

            @p.on("vector")
            def _(v, dtile=dtile, w=ye_last, xa=xa, c=vc4, zcn=zcn):
                v.wait_ge(s_sq.h, w)
                v.wait_ge(s_x3.h, xa)
                v.tensor_scalar(sA[:], z_sb[:, zcn, :], dv_sb[:, dtile:dtile + 1], None,
                                ALU.mult).then_inc(s_vch.h, 1)
                v.wait_ge(s_vch.h, c + 1)
                v.tensor_tensor(sA[:], sA[:], y_dt[:], ALU.add).then_inc(s_vch.h, 1)
                v.wait_ge(s_vch.h, c + 2)
                v.tensor_tensor(sA[:], sA[:], xown[:], ALU.add).then_inc(s_ev.h, 1)
            re_prev = s_ev.inc(1)

            @p.on("scalar")
            def _(s, dtile=dtile, w=re_prev):
                s.wait_ge(s_ev.h, w)
                s.dma_start(out=XO[dtile * 128:(dtile + 1) * 128, :], in_=sA[:])\
                    .then_inc(s_out.h, 16)
            s_out.inc(16)

        @p.on("scalar")
        def _(s, n=s_out.n):
            s.wait_ge(s_out.h, n)

        p.run(block)
    return nc


def run_attn(x_full, Wq, Wk, Wv, Wo):
    T2 = 2048
    idf = np.eye(128, dtype=np.float32)
    idb = np.eye(128, dtype=BF)
    wq3 = np.ascontiguousarray(np.asarray(Wq, np.float32).T.reshape(D, 16, 128).transpose(1, 0, 2))
    wk3 = np.ascontiguousarray(np.asarray(Wk, np.float32).T.reshape(D, 4, 128).transpose(1, 0, 2))
    wv3 = np.ascontiguousarray(np.asarray(Wv, np.float32).T.reshape(D, 4, 128).transpose(1, 0, 2))
    wo3 = np.ascontiguousarray(np.asarray(Wo, np.float32).T.reshape(D, 16, 128).transpose(1, 0, 2))
    inv_freq = 1.0 / (10000.0 ** (np.arange(0, 128, 2, dtype=np.float32) / 128.0))
    pos = np.arange(T2, dtype=np.float32)
    fr = pos[:, None] * inv_freq[None, :]
    cosd = np.ascontiguousarray(np.cos(fr).T)
    sind = np.ascontiguousarray(np.sin(fr).T)
    cosd = cosd.astype(BF)
    sind = sind.astype(BF)
    in_maps = []
    for c in range(NCORES):
        b, hh = divmod(c, 2)
        xT = x_full[b].T
        if hh == 1:
            xr = np.ascontiguousarray(np.concatenate([xT[:, 1024:], xT[:, 0:1024]], axis=1))
            cr = np.ascontiguousarray(np.concatenate([cosd[:, 1024:], cosd[:, 0:1024]], axis=1))
            sr = np.ascontiguousarray(np.concatenate([sind[:, 1024:], sind[:, 0:1024]], axis=1))
        else:
            xr, cr, sr = np.ascontiguousarray(xT), cosd, sind
        in_maps.append({"xf": xr, "wq3": wq3, "wk3": wk3, "wv3": wv3, "wo3": wo3,
                        "cos": cr, "sin": sr, "idf": idf, "idb": idb})
    res = run_bass_kernel_spmd(_get_nc("attn", build_attn), in_maps,
                               core_ids=list(range(NCORES)))
    out = np.empty_like(x_full)
    for c in range(NCORES):
        b, hh = divmod(c, 2)
        out[b, hh * 1024:(hh + 1) * 1024, :] = res.results[c]["xo"].T
    return out


def run_mamba(x_full, x_proj_w, dt_proj_w, dt_proj_b, A_log, Dv):
    T2 = 2048
    idf = np.eye(128, dtype=np.float32)
    xpw = np.asarray(x_proj_w, np.float32)
    dtw = np.asarray(dt_proj_w, np.float32)
    dtb = np.asarray(dt_proj_b, np.float32)
    alog = np.asarray(A_log, np.float32)
    dv = np.asarray(Dv, np.float32)
    pidx = np.arange(128)
    pats = np.zeros((128, 16, 128), np.float32)
    for j in range(16):
        pats[j * 8 + pidx // 16, j, pidx] = 1.0
    pats = np.ascontiguousarray(pats.reshape(128, 2048).astype(BF))
    patbc = np.zeros((32, 256), np.float32)
    patbc[pidx % 16, pidx] = 1.0
    patbc[16 + pidx % 16, 128 + pidx] = 1.0
    patbc = patbc.astype(BF)
    pat8 = np.zeros((128, 8), np.float32)
    pat8[pidx, pidx // 16] = 1.0
    pat8 = pat8.astype(BF)
    in_maps = []
    for c in range(NCORES):
        b, hh = divmod(c, 2)
        perm = np.concatenate([np.arange(hh * 1024, (hh + 1) * 1024),
                               np.arange((1 - hh) * 1024, (2 - hh) * 1024)])
        xf = np.ascontiguousarray(x_full[b].T[perm])
        xpw_rot = xpw[:, perm]
        xpwd = np.ascontiguousarray(
            xpw_rot[0:128].T.reshape(16, 128, 128).transpose(1, 0, 2).reshape(128, 2048))
        xpwbc = np.ascontiguousarray(
            xpw_rot[128:160].T.reshape(16, 128, 32).transpose(1, 0, 2).reshape(128, 512))
        sl = slice(hh * 1024, (hh + 1) * 1024)
        dtpt = np.ascontiguousarray(dtw[sl].T)
        dtbv = np.ascontiguousarray(dtb[sl].reshape(8, 128).T)
        alogt = np.ascontiguousarray(alog[sl].reshape(128, 128).T)
        dvv = np.ascontiguousarray(dv[sl].reshape(8, 128).T)
        in_maps.append({"xf": xf, "xpwd": xpwd, "xpwbc": xpwbc, "dtpt": dtpt,
                        "dtb": dtbv, "alogt": alogt, "dvin": dvv,
                        "pats": pats, "patbc": patbc, "pat8": pat8,
                        "idf": idf, "hhc": np.array([float(hh)], np.float32)})
    res = run_bass_kernel_spmd(_get_nc("mamba", build_mamba), in_maps,
                               core_ids=list(range(NCORES)))
    out = np.empty_like(x_full)
    for c in range(NCORES):
        b, hh = divmod(c, 2)
        out[b, :, hh * 1024:(hh + 1) * 1024] = res.results[c]["xo"].T
    return out


def kernel(**inputs):
    x = np.asarray(inputs["L0_attn_hidden_states"], dtype=np.float32)
    x = run_ffn0(x, inputs["L0_ffn_w_gate_up"], inputs["L0_ffn_w_proj"])
    x = run_attn(x, inputs["L0_attn_Wq"], inputs["L0_attn_Wk"],
                 inputs["L0_attn_Wv"], inputs["L0_attn_Wo"])
    x = run_ffn0(x, inputs["L0_ffn_w_gate_up"], inputs["L0_ffn_w_proj"])
    x = run_ffn1(x, inputs["L1_ffn_W_base"], inputs["L1_ffn_b_base"],
                 inputs["L1_ffn_router_W"], inputs["L1_ffn_A"], inputs["L1_ffn_Bw"])
    x = run_mamba(x, inputs["L1_attn_x_proj_w"], inputs["L1_attn_dt_proj_w"],
                  inputs["L1_attn_dt_proj_b"], inputs["L1_attn_A_log"],
                  inputs["L1_attn_D"])
    x = run_ffn1(x, inputs["L1_ffn_W_base"], inputs["L1_ffn_b_base"],
                 inputs["L1_ffn_router_W"], inputs["L1_ffn_A"], inputs["L1_ffn_Bw"])
    return x
